# version 13
# baseline (speedup 1.0000x reference)
import numpy as np
import ml_dtypes
from contextlib import ExitStack

import concourse.bass as bass
import concourse.mybir as mybir
from concourse.bass_utils import run_bass_kernel_spmd

F32 = mybir.dt.float32
BF16 = mybir.dt.bfloat16
AF = mybir.ActivationFunctionType
OP = mybir.AluOpType
AX = mybir.AxisListType.X
BF = ml_dtypes.bfloat16

S = 2048
DM = 1024
NCH = DM // 128
H = 16
DK = 64
FFD = 4096
FCH = FFD // 128
NE = 8
NPAIR = 8
EPS = 1e-5
SCALE_QK = 0.125

ENGS = ("tensor", "vector", "scalar", "gpsimd", "sync")


class P:

    def __init__(self):
        self.ops = {e: [] for e in ENGS}
        self.n = {}
        self.waits = []

    def sem(self, *names):
        for nm in names:
            self.n.setdefault(nm, 0)

    def cur(self, nm):
        return self.n[nm]

    def w(self, eng, nm, val):
        if val <= 0:
            return
        self.waits.append((nm, val))
        self.ops[eng].append(("w", nm, val))

    def o(self, eng, f, inc=None):
        if inc is not None and not isinstance(inc, list):
            inc = [inc]
        if inc:
            for nm, amt in inc:
                self.n[nm] += amt
        self.ops[eng].append(("o", f, inc))

    def dma(self, eng, out, in_, nm="wl"):
        self.o(eng, lambda e, o=out, i=in_: e.dma_start(out=o, in_=i), inc=(nm, 16))
        return self.n[nm]

    def emit(self, nc, es):
        for nm, val in self.waits:
            assert val <= self.n[nm], f"wait {nm}>={val} exceeds total {self.n[nm]}"
        handles = {nm: es.enter_context(nc.semaphore(nm)) for nm in self.n}
        block = es.enter_context(nc.Block())

        def runner(ops):
            def run(e):
                for item in ops:
                    if item[0] == "w":
                        e.wait_ge(handles[item[1]], item[2])
                    else:
                        _, f, inc = item
                        ins = f(e)
                        if inc:
                            for nm, amt in inc:
                                ins.then_inc(handles[nm], amt)
            return run

        for eng in ENGS:
            if self.ops[eng]:
                getattr(block, eng)(runner(self.ops[eng]))


def fm(buf, c, t0=0, t1=S):
    return buf[:, c * S + t0: c * S + t1]


def emit_ln_stats(p, B, src=None, ob=None):
    xf32, xb16 = B["xf32"], B["xb16"]
    src = src if src is not None else xf32
    ob = ob if ob is not None else xb16
    pA, sq16 = B["pA"], B["sq16"]
    onesk = B["onesk"]
    st = B["lnst"]

    p.w("scalar", "ve", p.cur("ve"))
    p.w("scalar", "pe", p.cur("pe"))
    for c in range(NCH):
        p.o("scalar", lambda e, c=c: e.activation(fm(ob, c), fm(src, c), AF.Copy),
            inc=("se", 1))
    se_cast = p.cur("se")
    for c in range(NCH):
        p.o("scalar", lambda e, c=c: e.activation(
            sq16[:, c * S:(c + 1) * S], fm(src, c), AF.Square), inc=("se", 1))
    se_sq = p.cur("se")

    p.w("tensor", "se", se_cast)
    for g in range(4):
        for c in range(NCH):
            p.o("tensor", lambda e, c=c, g=g: e.matmul(
                pA[0:1, g * 512:(g + 1) * 512], onesk[:, :],
                fm(ob, c, g * 512, (g + 1) * 512),
                start=(c == 0), stop=(c == NCH - 1)),
                inc=("pe", 1) if c == NCH - 1 else None)
    pe_sum = p.cur("pe")
    stA = st[0:1, 0:S]
    stB = st[0:1, S:2 * S]
    p.w("vector", "pe", pe_sum)
    p.o("vector", lambda e: e.tensor_scalar(stB, pA[0:1, :], 1.0 / DM, None,
                                            OP.mult), inc=("ve", 1))
    ve_mean = p.cur("ve")

    p.w("tensor", "ve", ve_mean)
    p.w("tensor", "se", se_sq)
    for g in range(4):
        for c in range(NCH):
            p.o("tensor", lambda e, c=c, g=g: e.matmul(
                pA[0:1, g * 512:(g + 1) * 512], onesk[:, :],
                sq16[:, c * S + g * 512: c * S + (g + 1) * 512],
                start=(c == 0), stop=(c == NCH - 1)),
                inc=("pe", 1) if c == NCH - 1 else None)
    pe_ssq = p.cur("pe")

    p.w("vector", "pe", pe_ssq)
    p.w("vector", "ve", ve_mean)
    p.o("vector", lambda e: e.tensor_tensor(stA, stB, stB, OP.mult),
        inc=("ve", 1))
    p.w("vector", "ve", p.cur("ve"))
    p.o("vector", lambda e: e.scalar_tensor_tensor(
        stA, pA[0:1, :], 1.0 / DM, stA, OP.mult, OP.subtract), inc=("ve", 1))
    ve_var = p.cur("ve")
    p.w("scalar", "ve", ve_var)
    p.o("scalar", lambda e: e.activation(pA[0:1, :], stA, AF.Ln,
                                         bias=B["epsv"][:, :]), inc=("se", 1))
    p.w("scalar", "se", p.cur("se"))
    p.o("scalar", lambda e: e.activation(stA, pA[0:1, :], AF.Exp,
                                         scale=-0.5), inc=("se", 1))
    se_A = p.cur("se")
    p.w("vector", "se", se_A)
    p.o("vector", lambda e: e.scalar_tensor_tensor(
        stB, stB, -1.0, stA, OP.mult, OP.mult), inc=("ve", 1))
    return p.cur("ve")


def emit_ln_apply(p, B, ve_B, mode="stream", src=None, ob=None, g0=0, g1=4):
    xf32, xb16 = B["xf32"], B["xb16"]
    src = src if src is not None else xf32
    ob = ob if ob is not None else xb16
    pA = B["pA"]
    onesf = B["onesf"]
    st = B["lnst"]
    tmp32 = B["tmp32"]
    for g in range(g0, g1):
        p.w("tensor", "ve", ve_B if g == g0 else p.cur("ve"))
        p.o("tensor", lambda e, g=g: e.matmul(
            pA[:, 0:512], onesf[:, :], st[0:1, g * 512:(g + 1) * 512],
            start=True, stop=True))
        p.o("tensor", lambda e, g=g: e.matmul(
            pA[:, 512:1024], onesf[:, :], st[0:1, S + g * 512:S + (g + 1) * 512],
            start=True, stop=True), inc=("pe", 1))
        pe_bc = p.cur("pe")
        p.w("vector", "pe", pe_bc)
        for c in range(NCH):
            p.o("vector", lambda e, c=c, g=g: e.tensor_tensor(
                tmp32[:, :], fm(src, c, g * 512, (g + 1) * 512), pA[:, 0:512],
                OP.mult), inc=("ve", 1))
            p.w("vector", "ve", p.cur("ve"))
            if mode == "b16":
                p.o("vector", lambda e, c=c, g=g: e.tensor_tensor(
                    fm(ob, c, g * 512, (g + 1) * 512), tmp32[:, :],
                    pA[:, 512:1024], OP.add), inc=("ve", 1))
            else:
                p.o("vector", lambda e, c=c, g=g: e.tensor_tensor(
                    fm(xf32, c, g * 512, (g + 1) * 512), tmp32[:, :],
                    pA[:, 512:1024], OP.add), inc=("ve", 1))
                if mode == "stream":
                    p.w("scalar", "ve", p.cur("ve"))
                    p.o("scalar", lambda e, c=c, g=g: e.activation(
                        fm(ob, c, g * 512, (g + 1) * 512),
                        fm(xf32, c, g * 512, (g + 1) * 512), AF.Copy),
                        inc=("se", 1))


def emit_ln(p, B, mode="stream", src=None, ob=None):
    ve_B = emit_ln_stats(p, B, src=src, ob=ob)
    emit_ln_apply(p, B, ve_B, mode=mode, src=src, ob=ob)


def emit_attention(p, B, layer):
    l1 = (layer == 1)
    xf32, xb16 = B["xf32"], B["xb16"]
    pA, pB, pT1, pT2 = B["pA"], B["pB"], B["pT1"], B["pT2"]
    qkvp, vtok = B["qkvp"], B["vtok"]
    e16, attnT, outT = B["e16"], B["attnT"], B["outT"]
    nmx, sums, pmx, stot = B["nmx"], B["sums"], B["pmx"], B["stot"]
    iden = B["iden"]
    wbuf, wpb = B["wbuf"], B["wpbuf"]
    bias = B["bias"]
    whbm, wp_hbm = B["wa_hbm"], B["wp_hbm"]
    if l1:
        cscs, scsc = B["cscs"], B["scsc"]
        maskneg = B["maskneg"]
        t1s, t2s = B["t1s"], B["t2s"]

    p.w("vector", "sl", B["sl_done"])
    p.o("vector", lambda e: e.tensor_scalar(bias[:, 0:8], bias[:, 0:8], SCALE_QK,
                                            None, OP.mult), inc=("ve", 1))
    p.w("scalar", "ve", p.cur("ve"))

    pair_mmdone = [0] * NPAIR
    pa_se = [0, 0, 0, 0]
    pa_ve = [0, 0, 0, 0]
    ptA_free = ("se", 0)
    ptB_free = ("se", 0)
    av_free = [0, 0]
    e16_free = [0, 0]
    e16_norm = [0, 0]
    ti = 0
    pend = None
    pend2 = None

    def emit_transp(t):
        nonlocal ptA_free, ptB_free
        pr, nk = t["pr"], t["nk"]
        eb = e16[:, pr * S:(pr + 1) * S]
        p.w("tensor", "ve", t["norm_ve"])
        t["tp_pe"] = []
        for half in range((nk + 7) // 8):
            j0, j1 = half * 8, min(nk, half * 8 + 8)
            pt = pT1 if half == 0 else pT2
            fr = ptA_free if half == 0 else ptB_free
            p.w("tensor", fr[0], fr[1])
            for j in range(j0, j1):
                p.o("tensor", lambda e, j=j, j0=j0, pt=pt, eb=eb:
                    e.transpose(pt[:, (j - j0) * 128:(j - j0 + 1) * 128],
                                eb[:, j * 128:(j + 1) * 128], iden[:, :]),
                    inc=("pe", 1) if j == j1 - 1 else None)
            t["tp_pe"].append(p.cur("pe"))
        e16_free[pr] = p.cur("pe")

    def emit_copy_half(t, half):
        nonlocal ptA_free, ptB_free
        pr, nk = t["pr"], t["nk"]
        at = attnT[:, pr * S:(pr + 1) * S]
        j0, j1 = half * 8, min(nk, half * 8 + 8)
        pt = pT1 if half == 0 else pT2
        p.w("scalar", "pe", t["tp_pe"][half])
        p.o("scalar", lambda e, j0=j0, j1=j1, pt=pt, at=at: e.activation(
            at[:, j0 * 128: j1 * 128], pt[:, 0:(j1 - j0) * 128], AF.Copy),
            inc=("se", 1))
        if half == 0:
            ptA_free = ("se", p.cur("se"))
            t["cpA_se"] = p.cur("se")
        else:
            ptB_free = ("se", p.cur("se"))
            t["cpB_se"] = p.cur("se")

    def emit_av(t):
        pr, nk, hg, qt = t["pr"], t["nk"], t["hg"], t["qt"]
        at = attnT[:, pr * S:(pr + 1) * S]
        avb = pr
        k1 = min(nk, 8)
        p.w("tensor", "se", max(t["cpA_se"], av_free[avb]))
        for kc in range(k1):
            p.o("tensor", lambda e, kc=kc, avb=avb, nk=nk, at=at, t=t:
                e.matmul(pB[0:64, avb * 512: avb * 512 + 128],
                         vtok[:, t["vbase"] + kc * 64: t["vbase"] + (kc + 1) * 64],
                         at[:, kc * 128:(kc + 1) * 128],
                         start=(kc == 0), stop=(kc == nk - 1)),
                inc=("pe", 1) if kc == nk - 1 else None)
        if nk > 8:
            p.w("tensor", "se", t["cpB_se"])
            for kc in range(8, nk):
                p.o("tensor", lambda e, kc=kc, avb=avb, nk=nk, at=at, t=t:
                    e.matmul(pB[0:64, avb * 512: avb * 512 + 128],
                             vtok[:, t["vbase"] + kc * 64: t["vbase"] + (kc + 1) * 64],
                             at[:, kc * 128:(kc + 1) * 128],
                             start=(kc == 0), stop=(kc == nk - 1)),
                    inc=("pe", 1) if kc == nk - 1 else None)
        p.w("scalar", "pe", p.cur("pe"))
        p.o("scalar", lambda e, hg=hg, qt=qt, avb=avb: e.activation(
            outT[(hg % 2) * 64:(hg % 2) * 64 + 64,
                 (hg // 2) * S + qt * 128:(hg // 2) * S + (qt + 1) * 128],
            pB[0:64, avb * 512: avb * 512 + 128], AF.Copy), inc=("se", 1))
        av_free[avb] = p.cur("se")

    def drain():
        nonlocal pend, pend2
        if pend is not None:
            emit_transp(pend)
            emit_copy_half(pend, 0)
            if pend["nk"] > 8:
                emit_copy_half(pend, 1)
            emit_av(pend)
        pend = pend2 = None

    nws = B.get("wslots", 1)
    for pp in range(NPAIR):
        if pp >= nws:
            p.w("gpsimd", "pe", pair_mmdone[pp - nws])
        wl = p.dma("gpsimd", wbuf[:, (pp % nws) * 3072:(pp % nws) * 3072 + 3072]
                   .rearrange("p (t k m) -> p t k m", t=3, k=8),
                   whbm[pp].rearrange("t k p m -> p t k m"), "wl")

        p.w("tensor", "wl", wl)
        for t3 in range(3):
            for gp in range(2):
                b0, b1 = 2 * gp, 2 * gp + 1
                p.w("tensor", "se", max(pa_se[b0], pa_se[b1]))
                p.w("tensor", "ve", max(pa_ve[b0], pa_ve[b1]))
                for kc in range(8):
                    for gi in range(2):
                        g = 2 * gp + gi
                        p.o("tensor", lambda e, pp=pp, t3=t3, g=g, kc=kc:
                            e.matmul(pA[:, g * 512:(g + 1) * 512],
                                     wbuf[:, (pp % nws) * 3072 + (t3 * 8 + kc) * 128:
                                          (pp % nws) * 3072 + (t3 * 8 + kc) * 128 + 128],
                                     fm(xb16, kc, g * 512, (g + 1) * 512),
                                     start=(kc == 0), stop=(kc == 7)),
                            inc=("pe", 1) if (kc == 7 and gi == 1) else None)
                pe_mm = p.cur("pe")
                p.w("scalar", "pe", pe_mm)
                sc = SCALE_QK if t3 == 0 else 1.0
                bcol = t3 * 8 + pp
                for g in (b0, b1):
                    p.o("scalar", lambda e, t3=t3, g=g, sc=sc, bcol=bcol:
                        e.activation(qkvp[:, t3 * S + g * 512: t3 * S + (g + 1) * 512],
                                     pA[:, g * 512:(g + 1) * 512], AF.Identity,
                                     bias=bias[:, bcol:bcol + 1], scale=sc),
                        inc=("se", 1))
                    pa_se[g] = p.cur("se")
        pair_mmdone[pp] = p.cur("pe")
        se_qkv = p.cur("se")

        drain()

        if l1:
            c1m, c2m = B["c1m"], B["c2m"]
            for t3 in range(2):
                off = t3 * S
                p.w("vector", "se", p.cur("se"))
                p.w("vector", "ve", p.cur("ve"))
                p.o("vector", lambda e, off=off: e.tensor_tensor(
                    t1s[:, :], qkvp[:, off:off + S], cscs[:, :], OP.mult),
                    inc=("ve", 1))
                p.o("vector", lambda e, off=off: e.tensor_tensor(
                    t2s[:, :], qkvp[:, off:off + S], scsc[:, :], OP.mult),
                    inc=("ve", 1))
                ve_t12 = p.cur("ve")
                p.w("tensor", "ve", ve_t12)
                p.w("tensor", "se", max(pa_se))
                p.w("tensor", "ve", max(pa_ve))
                for g in range(4):
                    p.o("tensor", lambda e, g=g: e.matmul(
                        pA[:, g * 512:(g + 1) * 512], c1m[:, :],
                        t1s[:, g * 512:(g + 1) * 512], start=True, stop=False))
                    p.o("tensor", lambda e, g=g: e.matmul(
                        pA[:, g * 512:(g + 1) * 512], c2m[:, :],
                        t2s[:, g * 512:(g + 1) * 512], start=False, stop=True),
                        inc=("pe", 1) if g == 3 else None)
                p.w("scalar", "pe", p.cur("pe"))
                for g in range(4):
                    p.o("scalar", lambda e, off=off, g=g:
                        e.activation(qkvp[:, off + g * 512: off + (g + 1) * 512],
                                     pA[:, g * 512:(g + 1) * 512],
                                     AF.Copy), inc=("se", 1))
                    pa_se[g] = p.cur("se")

        p.w("tensor", "se", se_qkv)
        for hh in range(2):
            for half in range(2):
                pt = pT1 if half == 0 else pT2
                fr = ptA_free if half == 0 else ptB_free
                p.w("tensor", fr[0], fr[1])
                for j in range(8):
                    kc = half * 8 + j
                    p.o("tensor", lambda e, hh=hh, kc=kc, j=j, pt=pt:
                        e.transpose(pt[:, j * 64:(j + 1) * 64],
                                    qkvp[hh * 64:(hh + 1) * 64,
                                         2 * S + kc * 128: 2 * S + (kc + 1) * 128],
                                    iden[hh * 64:(hh + 1) * 64,
                                         hh * 64:(hh + 1) * 64]),
                        inc=("pe", 1) if j == 7 else None)
                p.w("scalar", "pe", p.cur("pe"))
                p.o("scalar", lambda e, hh=hh, half=half, pt=pt: e.activation(
                    vtok[:, hh * 1024 + half * 512: hh * 1024 + (half + 1) * 512],
                    pt[:, 0:512], AF.Copy), inc=("se", 1))
                if half == 0:
                    ptA_free = ("se", p.cur("se"))
                else:
                    ptB_free = ("se", p.cur("se"))

        for hh in range(2):
            hg = 2 * pp + hh
            for qt in range(16):
                nk = (qt + 1) if l1 else 16
                ng = (nk * 128 + 511) // 512
                pr = ti % 2
                ti += 1
                eb = e16[:, pr * S:(pr + 1) * S]
                this = dict(pr=pr, hh=hh, qt=qt, nk=nk, hg=hg, norm_ve=None,
                            vbase=hh * 1024)
                p.w("tensor", "se", max(pa_se[g] for g in range(ng)))
                p.w("tensor", "ve", max(pa_ve[g] for g in range(ng)))
                pe_sc = []
                for g in range(ng):
                    n0, n1 = g * 512, min(nk * 128, (g + 1) * 512)
                    p.o("tensor", lambda e, hh=hh, qt=qt, n0=n0, n1=n1:
                        e.matmul(pA[:, n0:n1],
                                 qkvp[hh * 64:(hh + 1) * 64,
                                      qt * 128:(qt + 1) * 128],
                                 qkvp[hh * 64:(hh + 1) * 64, S + n0: S + n1],
                                 start=True, stop=True), inc=("pe", 1))
                    pe_sc.append(p.cur("pe"))
                pe_sc_all = p.cur("pe")
                if pend is not None:
                    emit_transp(pend)
                gd = (qt * 128) // 512 if l1 else -1
                for g in range(ng):
                    n0, n1 = g * 512, min(nk * 128, (g + 1) * 512)
                    p.w("vector", "pe", pe_sc[g])
                    if g == gd:
                        p.o("vector", lambda e, qt=qt: e.tensor_tensor(
                            pA[:, qt * 128:(qt + 1) * 128],
                            pA[:, qt * 128:(qt + 1) * 128], maskneg[:, :],
                            OP.add), inc=("ve", 1))
                        p.w("vector", "ve", p.cur("ve"))
                    p.o("vector", lambda e, g=g, n0=n0, n1=n1, pr=pr:
                        e.reduce_max(pmx[:, pr * 4 + g:pr * 4 + g + 1],
                                     pA[:, n0:n1], axis=AX), inc=("ve", 1))
                p.w("vector", "ve", p.cur("ve"))
                p.o("vector", lambda e, pr=pr, ng=ng: e.reduce_max(
                    nmx[:, pr * 2:pr * 2 + 1], pmx[:, pr * 4:pr * 4 + ng],
                    axis=AX, negate=True), inc=("ve", 1))
                ve_max = p.cur("ve")
                if pend is not None:
                    emit_copy_half(pend, 0)
                p.w("scalar", "ve", max(ve_max, e16_norm[pr]))
                p.w("scalar", "pe", e16_free[pr])
                p.o("scalar", lambda e, nk=nk, pr=pr, eb=eb:
                    e.activation(eb[:, 0:nk * 128], pA[:, 0:nk * 128], AF.Exp,
                                 bias=nmx[:, pr * 2:pr * 2 + 1],
                                 accum_out=sums[:, pr * 4:pr * 4 + 1]),
                    inc=("se", 1))
                se_exp = p.cur("se")
                for g in range(ng):
                    pa_se[g] = se_exp
                    pa_ve[g] = 0
                if pend is not None:
                    if pend["nk"] > 8:
                        emit_copy_half(pend, 1)
                    emit_av(pend)
                p.w("vector", "se", se_exp)
                p.o("vector", lambda e, pr=pr: e.reciprocal(
                    nmx[:, pr * 2 + 1:pr * 2 + 2], sums[:, pr * 4:pr * 4 + 1]),
                    inc=("ve", 1))
                p.w("vector", "ve", p.cur("ve"))
                p.o("vector", lambda e, nk=nk, pr=pr, eb=eb: e.tensor_scalar(
                    eb[:, 0:nk * 128], eb[:, 0:nk * 128],
                    nmx[:, pr * 2 + 1:pr * 2 + 2], None, OP.mult), inc=("ve", 1))
                this["norm_ve"] = p.cur("ve")
                e16_norm[pr] = this["norm_ve"]
                pend = this
    drain()

    se_attn_all = p.cur("se")
    ve_attn_all = p.cur("ve")
    proj_pe = [0] * 8
    for m in range(8):
        if m >= nws:
            p.w("gpsimd", "pe", proj_pe[m - nws])
        wl = p.dma("gpsimd", wpb[:, (m % nws) * 1024:(m % nws) * 1024 + 1024]
                   .rearrange("p (k c) -> p k c", k=8),
                   wp_hbm[m].rearrange("k p c -> p k c"), "wl")
        p.w("tensor", "wl", wl)
        if m == 0:
            p.w("tensor", "se", se_attn_all)
            p.w("tensor", "ve", ve_attn_all)
        for gp in range(2):
            b0, b1 = 2 * gp, 2 * gp + 1
            p.w("tensor", "se", max(pa_se[b0], pa_se[b1]))
            p.w("tensor", "ve", max(pa_ve[b0], pa_ve[b1]))
            for kc in range(8):
                for gi in range(2):
                    g = 2 * gp + gi
                    p.o("tensor", lambda e, m=m, g=g, kc=kc:
                        e.matmul(pA[:, g * 512:(g + 1) * 512],
                                 wpb[:, (m % nws) * 1024 + kc * 128:
                                     (m % nws) * 1024 + (kc + 1) * 128],
                                 outT[:, kc * S + g * 512: kc * S + (g + 1) * 512],
                                 start=(kc == 0), stop=(kc == 7)),
                        inc=("pe", 1) if (kc == 7 and gi == 1) else None)
            pe_mm = p.cur("pe")
            p.w("vector", "pe", pe_mm)
            for g in (b0, b1):
                p.o("vector", lambda e, m=m, g=g: e.scalar_tensor_tensor(
                    fm(xf32, m, g * 512, (g + 1) * 512),
                    pA[:, g * 512:(g + 1) * 512], bias[:, 24 + m:25 + m],
                    fm(xf32, m, g * 512, (g + 1) * 512), OP.add, OP.add),
                    inc=("ve", 1))
                pa_ve[g] = p.cur("ve")
        proj_pe[m] = p.cur("pe")
    p.w("tensor", "ve", p.cur("ve"))
    emit_ln(p, B, "stream")


def emit_l0_ffn(p, B):
    xf32, xb16 = B["xf32"], B["xb16"]
    pA = B["pA"]
    h0, h1 = B["h0"], B["h1bc"]
    a_scr = B["a_scr"]
    wbig = B["wbig"]
    scl = B["scl"]
    xsp = B["xsp_hbm"]
    w12_hbm, h1_hbm, h2_hbm, w3_hbm = (B["w12_hbm"], B["h1_hbm"], B["h2_hbm"],
                                       B["w3_hbm"])
    T = 1024
    ev = {}
    dma_hist = []

    def hq(buf, c):
        return buf[:, c * T:(c + 1) * T]

    for half in range(2):
        t0 = half * T
        p.w("sync", "ve", p.cur("ve"))
        p.w("sync", "se", p.cur("se"))
        sp_done = p.dma("sync", xsp, xf32[:, :], "sl")

        last_stt_ve = [0, 0]
        for m in range(FCH):
            if len(dma_hist) >= 2:
                p.w("gpsimd", "pe", dma_hist[-2])
            wl = p.dma("gpsimd", wbig[:, (m % 2) * 4096:(m % 2) * 4096 + 2048]
                       .rearrange("p (w k c) -> p w k c", w=2, k=8),
                       w12_hbm[m].rearrange("w k p c -> p w k c"), "wl")
            p.w("tensor", "wl", wl)
            b0 = (m % 2) * 2
            p.w("tensor", "se", max(ev.get(("s", b0), 0), ev.get(("s", b0 + 1), 0)))
            p.w("tensor", "ve", max(ev.get(("v", b0), 0), ev.get(("v", b0 + 1), 0)))
            for kc in range(8):
                for gi in range(2):
                    p.o("tensor", lambda e, m=m, kc=kc, b0=b0, gi=gi, t0=t0:
                        e.matmul(pA[:, (b0 + gi) * 512:(b0 + gi + 1) * 512],
                                 wbig[:, (m % 2) * 4096 + kc * 128:
                                      (m % 2) * 4096 + (kc + 1) * 128],
                                 fm(xb16, kc, t0 + gi * 512, t0 + (gi + 1) * 512),
                                 start=(kc == 0), stop=(kc == 7)),
                        inc=("pe", 1) if (kc == 7 and gi == 1) else None)
            p.w("scalar", "pe", p.cur("pe"))
            p.w("scalar", "ve", last_stt_ve[m % 2])
            for gi in range(2):
                p.o("scalar", lambda e, m=m, b0=b0, gi=gi: e.activation(
                    a_scr[:, (m % 2) * 1024 + gi * 512:(m % 2) * 1024 + (gi + 1) * 512],
                    pA[:, (b0 + gi) * 512:(b0 + gi + 1) * 512], AF.Silu,
                    scale=scl[:, m:m + 1]), inc=("se", 1))
            se_a = p.cur("se")
            ev[("s", b0)] = ev[("s", b0 + 1)] = se_a
            p.w("tensor", "se", se_a)
            for kc in range(8):
                for gi in range(2):
                    p.o("tensor", lambda e, m=m, kc=kc, b0=b0, gi=gi, t0=t0:
                        e.matmul(pA[:, (b0 + gi) * 512:(b0 + gi + 1) * 512],
                                 wbig[:, (m % 2) * 4096 + 1024 + kc * 128:
                                      (m % 2) * 4096 + 1024 + (kc + 1) * 128],
                                 fm(xb16, kc, t0 + gi * 512, t0 + (gi + 1) * 512),
                                 start=(kc == 0), stop=(kc == 7)),
                        inc=("pe", 1) if (kc == 7 and gi == 1) else None)
            pe_w2 = p.cur("pe")
            dma_hist.append(pe_w2)
            p.w("vector", "pe", pe_w2)
            p.w("vector", "se", se_a)
            for gi in range(2):
                p.o("vector", lambda e, m=m, b0=b0, gi=gi: e.scalar_tensor_tensor(
                    h0[:, m * T + gi * 512: m * T + (gi + 1) * 512],
                    pA[:, (b0 + gi) * 512:(b0 + gi + 1) * 512],
                    scl[:, 32 + m:33 + m],
                    a_scr[:, (m % 2) * 1024 + gi * 512:(m % 2) * 1024 + (gi + 1) * 512],
                    OP.mult, OP.mult), inc=("ve", 1))
            ve_h0 = p.cur("ve")
            ev[("v", b0)] = ev[("v", b0 + 1)] = ve_h0
            last_stt_ve[m % 2] = ve_h0

        for phase, (hbm, srcb, dstb, scol) in enumerate(
                ((h1_hbm, h0, h1, 64), (h2_hbm, h1, h0, 96))):
            src_cnt = p.cur("ve") if phase == 0 else p.cur("se")
            src_eng = "ve" if phase == 0 else "se"
            for m in range(FCH):
                if len(dma_hist) >= 2:
                    p.w("gpsimd", "pe", dma_hist[-2])
                wl = p.dma("gpsimd",
                           wbig[:, (m % 2) * 4096:(m % 2) * 4096 + 4096]
                           .rearrange("p (k c) -> p k c", k=32),
                           hbm[m].rearrange("k p c -> p k c"), "wl")
                p.w("tensor", "wl", wl)
                b0 = (m % 2) * 2
                p.w("tensor", "se", max(ev.get(("s", b0), 0), ev.get(("s", b0 + 1), 0)))
                p.w("tensor", "ve", max(ev.get(("v", b0), 0), ev.get(("v", b0 + 1), 0)))
                if m == 0:
                    p.w("tensor", src_eng, src_cnt)
                for kc in range(FCH):
                    for gi in range(2):
                        p.o("tensor", lambda e, m=m, kc=kc, b0=b0, gi=gi, srcb=srcb:
                            e.matmul(pA[:, (b0 + gi) * 512:(b0 + gi + 1) * 512],
                                     wbig[:, (m % 2) * 4096 + kc * 128:
                                          (m % 2) * 4096 + (kc + 1) * 128],
                                     srcb[:, kc * T + gi * 512: kc * T + (gi + 1) * 512],
                                     start=(kc == 0), stop=(kc == FCH - 1)),
                            inc=("pe", 1) if (kc == FCH - 1 and gi == 1) else None)
                pe_mm = p.cur("pe")
                dma_hist.append(pe_mm)
                p.w("scalar", "pe", pe_mm)
                if phase == 0 and m == 0:
                    p.w("scalar", "sl", sp_done)
                for gi in range(2):
                    p.o("scalar", lambda e, m=m, b0=b0, gi=gi, dstb=dstb, scol=scol:
                        e.activation(dstb[:, m * T + gi * 512: m * T + (gi + 1) * 512],
                                     pA[:, (b0 + gi) * 512:(b0 + gi + 1) * 512],
                                     AF.Gelu, scale=scl[:, scol + m:scol + m + 1]),
                        inc=("se", 1))
                ev[("s", b0)] = ev[("s", b0 + 1)] = p.cur("se")

        p.w("sync", "pe", p.cur("pe"))
        rs_done = p.dma("sync", xf32[:, :], xsp, "sl")

        se_h2 = p.cur("se")
        for m in range(8):
            if len(dma_hist) >= 2:
                p.w("gpsimd", "pe", dma_hist[-2])
            wl = p.dma("gpsimd", wbig[:, (m % 2) * 4096:(m % 2) * 4096 + 4096]
                       .rearrange("p (k c) -> p k c", k=32),
                       w3_hbm[m].rearrange("k p c -> p k c"), "wl")
            p.w("tensor", "wl", wl)
            b0 = (m % 2) * 2
            p.w("tensor", "se", max(ev.get(("s", b0), 0), ev.get(("s", b0 + 1), 0)))
            p.w("tensor", "ve", max(ev.get(("v", b0), 0), ev.get(("v", b0 + 1), 0)))
            if m == 0:
                p.w("tensor", "se", se_h2)
            for kc in range(FCH):
                for gi in range(2):
                    p.o("tensor", lambda e, m=m, kc=kc, b0=b0, gi=gi:
                        e.matmul(pA[:, (b0 + gi) * 512:(b0 + gi + 1) * 512],
                                 wbig[:, (m % 2) * 4096 + kc * 128:
                                      (m % 2) * 4096 + (kc + 1) * 128],
                                 h0[:, kc * T + gi * 512: kc * T + (gi + 1) * 512],
                                 start=(kc == 0), stop=(kc == FCH - 1)),
                        inc=("pe", 1) if (kc == FCH - 1 and gi == 1) else None)
            pe_mm = p.cur("pe")
            dma_hist.append(pe_mm)
            p.w("vector", "pe", pe_mm)
            if m == 0:
                p.w("vector", "sl", rs_done)
            for gi in range(2):
                p.o("vector", lambda e, m=m, b0=b0, gi=gi, t0=t0:
                    e.scalar_tensor_tensor(
                        fm(xf32, m, t0 + gi * 512, t0 + (gi + 1) * 512),
                        pA[:, (b0 + gi) * 512:(b0 + gi + 1) * 512],
                        scl[:, 128 + m:129 + m],
                        fm(xf32, m, t0 + gi * 512, t0 + (gi + 1) * 512),
                        OP.mult, OP.add), inc=("ve", 1))
            ev[("v", b0)] = ev[("v", b0 + 1)] = p.cur("ve")

    p.w("tensor", "ve", p.cur("ve"))
    emit_ln(p, B, "stream")


def emit_l1_ffn(p, B):
    xf32, xb16 = B["xf32"], B["xb16"]
    pA, pB = B["pA"], B["pB"]
    rW, rb = B["rw_sb"], B["rb_sb"]
    onesf = B["onesf"]
    gtok, gT = B["gtok"], B["gT"]
    lgt, gsc = B["lgt"], B["gsc"]
    msk1, msk2 = B["msk1"], B["msk2"]
    lng, lnb = B["lng_sb"], B["lnb_sb"]
    b1t, b2 = B["b1_sb"], B["b2_sb"]
    sel = B["sel"]
    ae = B["ae"]
    bcG = B["bcG"]
    tmp32 = B["tmp32"]
    w1_hbm, w2_hbm = B["w1_hbm"], B["w2_hbm"]
    wbig = B["wbig"]
    T = 1024

    ev_rt = {}
    p.w("tensor", "sl", B["sl_done"])
    for tt in range(16):
        bank = tt % 4
        p.w("tensor", "ve", ev_rt.get(bank, 0))
        p.o("tensor", lambda e, bank=bank: e.matmul(
            pA[:, bank * 512: bank * 512 + 8], onesf[:, :], rb[:, :],
            start=True, stop=False))
        for kc in range(8):
            p.o("tensor", lambda e, tt=tt, kc=kc, bank=bank: e.matmul(
                pA[:, bank * 512: bank * 512 + 8],
                fm(xf32, kc, tt * 128, (tt + 1) * 128),
                rW[:, kc * 8:(kc + 1) * 8],
                start=False, stop=(kc == 7)),
                inc=("pe", 1) if kc == 7 else None)
        p.w("vector", "pe", p.cur("pe"))
        p.o("vector", lambda e, tt=tt, bank=bank: e.tensor_copy(
            lgt[:, tt * 8:(tt + 1) * 8], pA[:, bank * 512: bank * 512 + 8]),
            inc=("ve", 1))
        ev_rt[bank] = p.cur("ve")

    for tt in range(16):
        sl_ = slice(tt * 8, (tt + 1) * 8)
        p.w("vector", "ve", p.cur("ve"))
        p.o("vector", lambda e, sl_=sl_, tt=tt: e.reduce_max(
            gsc[:, tt:tt + 1], lgt[:, sl_], axis=AX), inc=("ve", 1))
        p.w("vector", "ve", p.cur("ve"))
        p.o("vector", lambda e, sl_=sl_, tt=tt: e.tensor_scalar(
            msk1[:, sl_], lgt[:, sl_], gsc[:, tt:tt + 1], None, OP.is_ge),
            inc=("ve", 1))
        p.w("vector", "ve", p.cur("ve"))
        p.o("vector", lambda e, sl_=sl_, tt=tt: e.scalar_tensor_tensor(
            msk2[:, sl_], msk1[:, sl_], -1e9, lgt[:, sl_], OP.mult, OP.add),
            inc=("ve", 1))
        p.w("vector", "ve", p.cur("ve"))
        p.o("vector", lambda e, sl_=sl_, tt=tt: e.reduce_max(
            gsc[:, 16 + tt:17 + tt], msk2[:, sl_], axis=AX), inc=("ve", 1))
    ve_m2 = p.cur("ve")
    p.w("vector", "ve", ve_m2)
    p.o("vector", lambda e: e.tensor_tensor(
        gsc[:, 32:48], gsc[:, 16:32], gsc[:, 0:16], OP.subtract), inc=("ve", 1))
    p.w("scalar", "ve", p.cur("ve"))
    p.o("scalar", lambda e: e.activation(gsc[:, 48:64], gsc[:, 32:48],
                                         AF.Sigmoid), inc=("se", 1))
    p.w("vector", "se", p.cur("se"))
    p.o("vector", lambda e: e.tensor_scalar(
        gsc[:, 32:48], gsc[:, 48:64], -1.0, 1.0, OP.mult, OP.add),
        inc=("ve", 1))
    for tt in range(16):
        sl_ = slice(tt * 8, (tt + 1) * 8)
        p.w("vector", "ve", p.cur("ve"))
        p.o("vector", lambda e, sl_=sl_, tt=tt: e.tensor_scalar(
            msk2[:, sl_], msk2[:, sl_], gsc[:, 16 + tt:17 + tt], None, OP.is_ge),
            inc=("ve", 1))
        p.w("vector", "ve", p.cur("ve"))
        p.o("vector", lambda e, sl_=sl_, tt=tt: e.tensor_scalar(
            gtok[:, sl_], msk1[:, sl_], gsc[:, 32 + tt:33 + tt], None, OP.mult),
            inc=("ve", 1))
        p.w("vector", "ve", p.cur("ve"))
        p.o("vector", lambda e, sl_=sl_, tt=tt: e.scalar_tensor_tensor(
            gtok[:, sl_], msk2[:, sl_], gsc[:, 48 + tt:49 + tt], gtok[:, sl_],
            OP.mult, OP.add), inc=("ve", 1))
    ve_gates = p.cur("ve")

    gtok16, iden16, pT1 = B["gtok16"], B["iden"], B["pT1"]
    p.w("vector", "ve", ve_gates)
    p.o("vector", lambda e: e.tensor_copy(gtok16[:, :], gtok[:, :]),
        inc=("ve", 1))
    p.w("tensor", "ve", p.cur("ve"))
    cp_done = 0
    for tt in range(16):
        p.w("tensor", "ve", cp_done)
        p.o("tensor", lambda e, tt=tt: e.transpose(
            pT1[0:8, 0:128], gtok16[:, tt * 8:(tt + 1) * 8], iden16[:, :]),
            inc=("pe", 1))
        p.w("vector", "pe", p.cur("pe"))
        p.o("vector", lambda e, tt=tt: e.tensor_copy(
            gT[:, tt * 128:(tt + 1) * 128], pT1[0:8, 0:128]), inc=("ve", 1))
        cp_done = p.cur("ve")

    ve_stats = emit_ln_stats(p, B)

    he_w1_pe = 0
    ev = {}
    dma_hist = []
    last_bcg_ve = 0
    for half in range(2):
        t0 = half * T
        g0 = half * 2
        oh = 1 - half

        def hv(kc, lo, hi, oh=oh):
            return xb16[:, kc * S + oh * 1024 + lo: kc * S + oh * 1024 + hi]

        p.w("vector", "pe", he_w1_pe)
        emit_ln_apply(p, B, ve_stats, mode="b16", g0=g0, g1=g0 + 2)
        ve_xhat = p.cur("ve")

        for c in range(NCH):
            for g in (g0, g0 + 1):
                bank = (c * 2 + g) % 4
                p.w("tensor", "ve", max(ev.get(("v", bank), 0), ve_xhat))
                p.w("tensor", "se", ev.get(("s", bank), 0))
                p.o("tensor", lambda e, c=c, g=g, bank=bank: e.matmul(
                    pA[:, bank * 512:(bank + 1) * 512],
                    b2[:, c * 128:(c + 1) * 128],
                    gT[:, g * 512:(g + 1) * 512], start=True, stop=True),
                    inc=("pe", 1))
                p.w("vector", "pe", p.cur("pe"))
                p.o("vector", lambda e, c=c, g=g, bank=bank:
                    e.scalar_tensor_tensor(
                        fm(xf32, c, g * 512, (g + 1) * 512),
                        fm(xf32, c, g * 512, (g + 1) * 512), 2.0,
                        pA[:, bank * 512:(bank + 1) * 512], OP.mult, OP.add),
                    inc=("ve", 1))
                ev[("v", bank)] = p.cur("ve")

        def emit_he(ex, t0=t0, oh=oh):
            p.w("vector", "pe", he_w1_pe)
            for c in range(NCH):
                p.o("vector", lambda e, c=c, ex=ex, oh=oh: e.tensor_scalar(
                    xb16[:, c * S + oh * 1024: c * S + oh * 1024 + 1024],
                    fm(xb16, c, t0, t0 + T),
                    lng[:, c * 8 + ex:c * 8 + ex + 1],
                    lnb[:, c * 8 + ex:c * 8 + ex + 1], OP.mult, OP.add),
                    inc=("ve", 1))
            return p.cur("ve")

        ve_he = emit_he(0)
        for ex in range(NE):
            p.w("tensor", "ve", p.cur("ve"))
            for gi in range(2):
                p.o("tensor", lambda e, ex=ex, t0=t0, gi=gi: e.matmul(
                    pB[:, gi * 512:(gi + 1) * 512],
                    sel[:, ex * 128:(ex + 1) * 128],
                    gT[:, t0 + gi * 512: t0 + (gi + 1) * 512],
                    start=True, stop=True),
                    inc=("pe", 1) if gi == 1 else None)
            p.w("vector", "pe", p.cur("pe"))
            for gi in range(2):
                p.o("vector", lambda e, gi=gi: e.tensor_copy(
                    bcG[:, gi * 512:(gi + 1) * 512],
                    pB[:, gi * 512:(gi + 1) * 512]), inc=("ve", 1))
            ve_bcg = p.cur("ve")
            last_bcg_ve = ve_bcg

            p.w("tensor", "ve", ve_he)
            for mb in range(8):
                if len(dma_hist) >= 2:
                    p.w("gpsimd", "pe", dma_hist[-2])
                wl = p.dma("gpsimd",
                           wbig[:, (mb % 2) * 4096:(mb % 2) * 4096 + 4096]
                           .rearrange("p (m k c) -> p m k c", m=4, k=8),
                           w1_hbm[ex, mb * 4:(mb + 1) * 4]
                           .rearrange("m k p c -> p m k c"), "wl")
                p.w("tensor", "wl", wl)
                for mi in range(4):
                    m = mb * 4 + mi
                    b0 = (m % 2) * 2
                    p.w("tensor", "se", max(ev.get(("s", b0), 0),
                                            ev.get(("s", b0 + 1), 0)))
                    p.w("tensor", "ve", max(ev.get(("v", b0), 0),
                                            ev.get(("v", b0 + 1), 0)))
                    for kc in range(8):
                        for gi in range(2):
                            p.o("tensor", lambda e, mb=mb, mi=mi, kc=kc, b0=b0,
                                gi=gi, hv=hv:
                                e.matmul(pA[:, (b0 + gi) * 512:(b0 + gi + 1) * 512],
                                         wbig[:, (mb % 2) * 4096 + mi * 1024 + kc * 128:
                                              (mb % 2) * 4096 + mi * 1024 + (kc + 1) * 128],
                                         hv(kc, gi * 512, (gi + 1) * 512),
                                         start=(kc == 0), stop=(kc == 7)),
                                inc=("pe", 1) if (kc == 7 and gi == 1) else None)
                    pe_mm = p.cur("pe")
                    p.w("scalar", "pe", pe_mm)
                    for gi in range(2):
                        p.o("scalar", lambda e, m=m, ex=ex, b0=b0, gi=gi:
                            e.activation(
                                ae[:, m * T + gi * 512: m * T + (gi + 1) * 512],
                                pA[:, (b0 + gi) * 512:(b0 + gi + 1) * 512],
                                AF.Gelu,
                                bias=b1t[:, ex * 32 + m:ex * 32 + m + 1]),
                            inc=("se", 1))
                    ev[("s", b0)] = ev[("s", b0 + 1)] = p.cur("se")
                dma_hist.append(p.cur("pe"))
            se_ae = p.cur("se")
            he_w1_pe = p.cur("pe")
            if ex + 1 < NE:
                ve_he = emit_he(ex + 1)

            p.w("tensor", "se", se_ae)
            p.w("tensor", "ve", ve_bcg)
            for m in range(NCH):
                if len(dma_hist) >= 2:
                    p.w("gpsimd", "pe", dma_hist[-2])
                wl = p.dma("gpsimd",
                           wbig[:, (m % 2) * 4096:(m % 2) * 4096 + 4096]
                           .rearrange("p (k c) -> p k c", k=32),
                           w2_hbm[ex, m].rearrange("k p c -> p k c"), "wl")
                p.w("tensor", "wl", wl)
                b0 = (m % 2) * 2
                p.w("tensor", "se", max(ev.get(("s", b0), 0),
                                        ev.get(("s", b0 + 1), 0)))
                p.w("tensor", "ve", max(ev.get(("v", b0), 0),
                                        ev.get(("v", b0 + 1), 0)))
                for kc in range(FCH):
                    for gi in range(2):
                        p.o("tensor", lambda e, m=m, kc=kc, b0=b0, gi=gi:
                            e.matmul(pA[:, (b0 + gi) * 512:(b0 + gi + 1) * 512],
                                     wbig[:, (m % 2) * 4096 + kc * 128:
                                          (m % 2) * 4096 + (kc + 1) * 128],
                                     ae[:, kc * T + gi * 512: kc * T + (gi + 1) * 512],
                                     start=(kc == 0), stop=(kc == FCH - 1)),
                            inc=("pe", 1) if (kc == FCH - 1 and gi == 1) else None)
                pe_mm = p.cur("pe")
                dma_hist.append(pe_mm)
                p.w("vector", "pe", pe_mm)
                for gi in range(2):
                    p.o("vector", lambda e, m=m, b0=b0, gi=gi: e.tensor_tensor(
                        pB[:, gi * 512:(gi + 1) * 512],
                        pA[:, (b0 + gi) * 512:(b0 + gi + 1) * 512],
                        bcG[:, gi * 512:(gi + 1) * 512], OP.mult), inc=("ve", 1))
                    p.w("vector", "ve", p.cur("ve"))
                    p.o("vector", lambda e, m=m, gi=gi, t0=t0: e.tensor_tensor(
                        fm(xf32, m, t0 + gi * 512, t0 + (gi + 1) * 512),
                        fm(xf32, m, t0 + gi * 512, t0 + (gi + 1) * 512),
                        pB[:, gi * 512:(gi + 1) * 512], OP.add), inc=("ve", 1))
                ev[("v", b0)] = ev[("v", b0 + 1)] = p.cur("ve")

    p.w("tensor", "ve", p.cur("ve"))
    emit_ln(p, B, "f32")


def alloc_common(nc, es, B):
    B["nc"] = nc
    B["xf32"] = es.enter_context(nc.sbuf_tensor("xf32", [128, NCH * S], F32))
    B["xb16"] = es.enter_context(nc.sbuf_tensor("xb16", [128, NCH * S], BF16))
    B["pA"] = es.enter_context(nc.psum_tensor("pA", [128, 2048], F32))
    B["pB"] = es.enter_context(nc.psum_tensor("pB", [128, 1024], F32))
    B["lnst"] = es.enter_context(nc.sbuf_tensor("lnst", [128, 2 * S], F32))
    B["onesk"] = es.enter_context(nc.sbuf_tensor("onesk", [128, 1], BF16))
    B["onesf"] = es.enter_context(nc.sbuf_tensor("onesf", [1, 128], F32))
    B["epsv"] = es.enter_context(nc.sbuf_tensor("epsv", [1, 1], F32))


def emit_prelude(p, B, xin_hbm):
    p.sem("sl", "wl", "pe", "ve", "se", "st")
    p.o("vector", lambda e: e.memset(B["onesk"][:, :], 1.0))
    p.o("vector", lambda e: e.memset(B["epsv"][:, :], EPS))
    p.o("vector", lambda e: e.memset(B["onesf"][:, :], 1.0), inc=("ve", 1))
    xf, xb = B["xf32"], B["xb16"]
    p.dma("sync", xf[:, :].rearrange("p (c t) -> p c t", c=NCH),
          xin_hbm.rearrange("(c p) t -> p c t", p=128), "sl")
    p.dma("gpsimd", xb[:, :].rearrange("p (c t) -> p c t", c=NCH),
          xin_hbm.rearrange("(c p) t -> p c t", p=128), "sl")
    sl_x = p.cur("sl")
    p.w("tensor", "sl", sl_x)
    p.w("tensor", "ve", 1)
    p.w("vector", "sl", sl_x)
    p.w("scalar", "sl", sl_x)


def emit_store(p, B, xout_hbm, src=None):
    src = src if src is not None else B["xf32"]
    p.w("sync", "ve", p.cur("ve"))
    p.w("sync", "se", p.cur("se"))
    p.dma("sync", xout_hbm.rearrange("(c p) t -> p c t", p=128),
          src[:, :].rearrange("p (c t) -> p c t", c=NCH), "st")
    p.w("sync", "st", p.cur("st"))



def build_attn(layer):
    l1 = (layer == 1)
    nc = bass.Bass("TRN2", target_bir_lowering=False)
    xin = nc.dram_tensor("xin", [DM, S], F32, kind="ExternalInput").ap()
    wa = nc.dram_tensor("wa", [NPAIR, 3, 8, 128, 128], F32,
                        kind="ExternalInput").ap()
    wp = nc.dram_tensor("wp", [8, 8, 128, 128], F32, kind="ExternalInput").ap()
    bias_h = nc.dram_tensor("biast", [128, 32], F32, kind="ExternalInput").ap()
    iden_h = nc.dram_tensor("iden", [128, 128], BF16, kind="ExternalInput").ap()
    if l1:
        cscs_h = nc.dram_tensor("cscs", [128, S], BF16, kind="ExternalInput").ap()
        scsc_h = nc.dram_tensor("scsc", [128, S], BF16, kind="ExternalInput").ap()
        mneg_h = nc.dram_tensor("mneg", [128, 128], F32, kind="ExternalInput").ap()
        c1_h = nc.dram_tensor("c1m", [128, 128], BF16, kind="ExternalInput").ap()
        c2_h = nc.dram_tensor("c2m", [128, 128], BF16, kind="ExternalInput").ap()
    xout = nc.dram_tensor("xout", [DM, S], F32, kind="ExternalOutput").ap()

    es = ExitStack()
    B = {}
    alloc_common(nc, es, B)
    sb = nc.sbuf_tensor
    B["qkvp"] = es.enter_context(sb("qkvp", [128, 3 * S], BF16))
    B["vtok"] = es.enter_context(sb("vtok", [128, 2048], BF16))
    B["e16"] = es.enter_context(sb("e16", [128, 2 * S], BF16))
    B["attnT"] = es.enter_context(sb("attnT", [128, 2 * S], BF16))
    B["outT"] = es.enter_context(sb("outT", [128, NCH * S], BF16))
    B["sq16"] = B["outT"]
    B["tmp32"] = B["outT"][:, 0:1024].bitcast(F32)
    B["nmx"] = es.enter_context(sb("nmx", [128, 4], F32))
    B["sums"] = es.enter_context(sb("sums", [128, 8], F32))
    B["pmx"] = es.enter_context(sb("pmx", [128, 8], F32))
    B["stot"] = es.enter_context(sb("stot", [128, 2], F32))
    B["iden"] = es.enter_context(sb("idens", [128, 128], BF16))
    B["wbuf"] = es.enter_context(sb("wbuf", [128, 2 * 3072], BF16))
    B["wpbuf"] = es.enter_context(sb("wpbuf", [128, 2 * 1024], BF16))
    B["bias"] = es.enter_context(sb("biass", [128, 32], F32))
    B["pT1"] = es.enter_context(nc.psum_tensor("pT1", [128, 1024], BF16))
    B["pT2"] = es.enter_context(nc.psum_tensor("pT2", [128, 1024], BF16))
    if l1:
        B["cscs"] = es.enter_context(sb("cscss", [128, S], BF16))
        B["scsc"] = es.enter_context(sb("scscs", [128, S], BF16))
        B["maskneg"] = es.enter_context(sb("mnegs", [128, 128], F32))
        B["t1s"] = B["e16"][:, 0:S]
        B["t2s"] = B["e16"][:, S:2 * S]
        B["c1m"] = es.enter_context(sb("c1ms", [128, 128], BF16))
        B["c2m"] = es.enter_context(sb("c2ms", [128, 128], BF16))
    B["wa_hbm"] = wa
    B["wp_hbm"] = wp

    p = P()
    emit_prelude(p, B, xin)
    p.dma("sync", B["iden"][:, :], iden_h, "sl")
    p.dma("sync", B["bias"][:, :], bias_h, "sl")
    if l1:
        p.dma("sync", B["cscs"][:, :], cscs_h, "sl")
        p.dma("sync", B["scsc"][:, :], scsc_h, "sl")
        p.dma("sync", B["maskneg"][:, :], mneg_h, "sl")
        p.dma("sync", B["c1m"][:, :], c1_h, "sl")
        p.dma("sync", B["c2m"][:, :], c2_h, "sl")
    B["sl_done"] = p.cur("sl")
    p.w("tensor", "sl", B["sl_done"])
    emit_attention(p, B, layer)
    emit_store(p, B, xout)
    p.emit(nc, es)
    es.close()
    return nc


def build_l0_ffn():
    nc = bass.Bass("TRN2", target_bir_lowering=False)
    xin = nc.dram_tensor("xin", [DM, S], F32, kind="ExternalInput").ap()
    w12 = nc.dram_tensor("w12", [FCH, 2, 8, 128, 128], F32,
                         kind="ExternalInput").ap()
    h1w = nc.dram_tensor("h1w", [FCH, FCH, 128, 128], F32,
                         kind="ExternalInput").ap()
    h2w = nc.dram_tensor("h2w", [FCH, FCH, 128, 128], F32,
                         kind="ExternalInput").ap()
    w3w = nc.dram_tensor("w3w", [8, FCH, 128, 128], F32,
                         kind="ExternalInput").ap()
    scl_h = nc.dram_tensor("scl", [128, 144], F32, kind="ExternalInput").ap()
    xout = nc.dram_tensor("xout", [DM, S], F32, kind="ExternalOutput").ap()

    es = ExitStack()
    B = {}
    alloc_common(nc, es, B)
    sb = nc.sbuf_tensor
    h0h = es.enter_context(sb("h0", [128, FCH * 1024], BF16))
    B["h0"] = h0h[:, :]
    B["h1bc"] = B["xf32"][:, :].bitcast(BF16)
    B["sq16"] = h0h[:, 0:NCH * S]
    B["tmp32"] = h0h[:, 0:1024].bitcast(F32)
    B["a_scr"] = B["lnst"][:, :].bitcast(BF16)[:, 0:2048]
    B["wbig"] = es.enter_context(sb("wbig", [128, 2 * 4096], BF16))
    B["scl"] = es.enter_context(sb("scls", [128, 144], F32))
    B["xsp_hbm"] = nc.dram_tensor("xsp", [128, NCH * S], F32,
                                  kind="Internal").ap()
    B["w12_hbm"], B["h1_hbm"], B["h2_hbm"], B["w3_hbm"] = w12, h1w, h2w, w3w

    p = P()
    emit_prelude(p, B, xin)
    p.dma("sync", B["scl"][:, :], scl_h, "sl")
    B["sl_done"] = p.cur("sl")
    p.w("tensor", "sl", B["sl_done"])
    p.w("scalar", "sl", B["sl_done"])
    p.w("vector", "sl", B["sl_done"])
    emit_l0_ffn(p, B)
    emit_store(p, B, xout)
    p.emit(nc, es)
    es.close()
    return nc


def build_l1_ffn():
    nc = bass.Bass("TRN2", target_bir_lowering=False)
    xin = nc.dram_tensor("xin", [DM, S], F32, kind="ExternalInput").ap()
    w1 = nc.dram_tensor("w1", [NE, FCH, 8, 128, 128], F32,
                        kind="ExternalInput").ap()
    w2 = nc.dram_tensor("w2", [NE, NCH, FCH, 128, 128], F32,
                        kind="ExternalInput").ap()
    rw_h = nc.dram_tensor("rw", [128, 64], F32, kind="ExternalInput").ap()
    rb_h = nc.dram_tensor("rb", [1, 8], F32, kind="ExternalInput").ap()
    lng_h = nc.dram_tensor("lng", [128, 64], F32, kind="ExternalInput").ap()
    lnb_h = nc.dram_tensor("lnb", [128, 64], F32, kind="ExternalInput").ap()
    b1_h = nc.dram_tensor("b1", [128, 256], F32, kind="ExternalInput").ap()
    b2_h = nc.dram_tensor("b2", [8, 1024], F32, kind="ExternalInput").ap()
    sel_h = nc.dram_tensor("sel", [8, 1024], F32, kind="ExternalInput").ap()
    idf_h = nc.dram_tensor("idf", [128, 128], F32, kind="ExternalInput").ap()
    xout = nc.dram_tensor("xout", [DM, S], F32, kind="ExternalOutput").ap()

    es = ExitStack()
    B = {}
    alloc_common(nc, es, B)
    sb = nc.sbuf_tensor
    B["rw_sb"] = es.enter_context(sb("rws", [128, 64], F32))
    B["rb_sb"] = es.enter_context(sb("rbs", [1, 8], F32))
    B["lgt"] = es.enter_context(sb("lgt", [128, 128], F32))
    B["gtok"] = B["lgt"][:, :]
    B["gsc"] = es.enter_context(sb("gsc", [128, 64], F32))
    B["msk1"] = es.enter_context(sb("msk1", [128, 128], BF16))
    B["msk2"] = es.enter_context(sb("msk2", [128, 128], BF16))
    B["gtok16"] = B["msk1"][:, :]
    B["iden"] = es.enter_context(sb("iden16", [128, 128], BF16))
    B["lng_sb"] = es.enter_context(sb("lngs", [128, 64], F32))
    B["lnb_sb"] = es.enter_context(sb("lnbs", [128, 64], F32))
    B["b1_sb"] = es.enter_context(sb("b1s", [128, 256], F32))
    aeh = es.enter_context(sb("ae", [128, FCH * 1024], BF16))
    B["ae"] = aeh[:, :]
    B["sq16"] = aeh[:, 0:NCH * S]
    B["tmp32"] = aeh[:, 0:1024].bitcast(F32)
    B["bcG"] = es.enter_context(sb("bcG", [128, 512], F32))[:, :].bitcast(BF16)
    B["wbig"] = es.enter_context(sb("wbig", [128, 2 * 4096], BF16))
    B["pT1"] = es.enter_context(nc.psum_tensor("pT1", [128, 1024], BF16))
    GTSl = es.enter_context(sb("GTS", [128, 4096], BF16))
    B["gT"] = GTSl[0:8, 0:2048]
    B["sel"] = GTSl[0:8, 2048:3072]
    B["b2_sb"] = GTSl[0:8, 3072:4096]
    B["w1_hbm"], B["w2_hbm"] = w1, w2

    p = P()
    emit_prelude(p, B, xin)
    for nm, hb in (("rw_sb", rw_h), ("rb_sb", rb_h), ("lng_sb", lng_h),
                   ("lnb_sb", lnb_h), ("b1_sb", b1_h)):
        p.dma("sync", B[nm][:, :], hb, "sl")
    p.dma("gpsimd", B["iden"][:, :], idf_h, "sl")
    p.dma("gpsimd", B["b2_sb"], b2_h, "sl")
    p.dma("gpsimd", B["sel"], sel_h, "sl")
    B["sl_done"] = p.cur("sl")
    p.w("tensor", "sl", B["sl_done"])
    p.w("scalar", "sl", B["sl_done"])
    p.w("vector", "sl", B["sl_done"])
    emit_l1_ffn(p, B)
    emit_store(p, B, xout)
    p.emit(nc, es)
    es.close()
    return nc



def _tile_kxm(w):
    K, M = w.shape
    return np.ascontiguousarray(
        w.reshape(K // 128, 128, M // 128, 128).transpose(2, 0, 1, 3))


def prep_l0_attn(I):
    w = np.asarray(I["L0_attn_w_qkv"])
    pairs = []
    for pp in range(NPAIR):
        cols = []
        for t3 in range(3):
            blk = w[:, t3 * 1024 + pp * 128: t3 * 1024 + (pp + 1) * 128]
            cols.append(blk.reshape(8, 128, 128))
        pairs.append(np.stack(cols))
    wa = np.ascontiguousarray(np.stack(pairs))
    wp = _tile_kxm(np.asarray(I["L0_attn_w_proj"]))
    bias = np.zeros((128, 32), np.float32)
    bq = np.asarray(I["L0_attn_b_qkv"]).reshape(24, 128).T
    bias[:, 0:24] = bq
    bias[:, 24:32] = np.asarray(I["L0_attn_b_proj"]).reshape(8, 128).T
    return {"wa": wa, "wp": wp, "biast": np.ascontiguousarray(bias),
            "iden": np.eye(128, dtype=BF)}


def _rope_tables():
    inv = 1.0 / 10000.0 ** (np.arange(0, DK, 2, dtype=np.float64) / DK)
    pos = np.arange(S, dtype=np.float64)
    fr = np.outer(pos, inv)
    c = np.cos(fr).T.astype(np.float32)
    s = np.sin(fr).T.astype(np.float32)
    cscs = np.concatenate([c, s, c, s]).astype(BF)
    scsc = np.concatenate([s, c, s, c]).astype(BF)
    return np.ascontiguousarray(cscs), np.ascontiguousarray(scsc)


def prep_l1_attn(I):
    perm = np.concatenate([h * 64 + np.concatenate(
        [np.arange(0, 64, 2), np.arange(1, 64, 2)]) for h in range(H)])
    Wq = np.asarray(I["L1_attn_Wq"])[:, perm]
    Wk = np.asarray(I["L1_attn_Wk"])[:, perm]
    Wv = np.asarray(I["L1_attn_Wv"])
    pairs = []
    for pp in range(NPAIR):
        cols = [W[:, pp * 128:(pp + 1) * 128].reshape(8, 128, 128)
                for W in (Wq, Wk, Wv)]
        pairs.append(np.stack(cols))
    wa = np.ascontiguousarray(np.stack(pairs))
    wp = _tile_kxm(np.asarray(I["L1_attn_Wo"]))
    bias = np.zeros((128, 32), np.float32)
    bias[:, 0:8] = np.asarray(I["L1_attn_bq"])[perm].reshape(8, 128).T
    bias[:, 8:16] = np.asarray(I["L1_attn_bk"])[perm].reshape(8, 128).T
    bias[:, 16:24] = np.asarray(I["L1_attn_bv"]).reshape(8, 128).T
    bias[:, 24:32] = np.asarray(I["L1_attn_bo"]).reshape(8, 128).T
    cscs, scsc = _rope_tables()
    mneg = np.triu(np.full((128, 128), -1e9, np.float32), 1).astype(BF)
    c1 = np.zeros((128, 128), np.float32)
    c2 = np.zeros((128, 128), np.float32)
    for b0 in (0, 64):
        for m in range(32):
            c1[b0 + m, b0 + m] = 1.0
            c1[b0 + 32 + m, b0 + m] = -1.0
            c2[b0 + m, b0 + 32 + m] = 1.0
            c2[b0 + 32 + m, b0 + 32 + m] = 1.0
    return {"wa": wa, "wp": wp, "biast": np.ascontiguousarray(bias),
            "iden": np.eye(128, dtype=BF), "cscs": cscs, "scsc": scsc,
            "mneg": np.ascontiguousarray(mneg),
            "c1m": np.ascontiguousarray(c1.astype(BF)),
            "c2m": np.ascontiguousarray(c2.astype(BF))}


def prep_l0_ffn(I):
    w1T = np.asarray(I["L0_ffn_w1_w"]).T
    w2T = np.asarray(I["L0_ffn_w2_w"]).T
    w12 = np.stack([_tile_kxm(w1T), _tile_kxm(w2T)], axis=1)
    h1w = _tile_kxm(np.asarray(I["L0_ffn_h1_w"]).T)
    h2w = _tile_kxm(np.asarray(I["L0_ffn_h2_w"]).T)
    w3w = _tile_kxm(np.asarray(I["L0_ffn_w3_w"]).T)
    scl = np.zeros((128, 144), np.float32)
    scl[:, 0:32] = np.asarray(I["L0_ffn_w1_s"]).reshape(32, 128).T
    scl[:, 32:64] = np.asarray(I["L0_ffn_w2_s"]).reshape(32, 128).T
    scl[:, 64:96] = np.asarray(I["L0_ffn_h1_s"]).reshape(32, 128).T
    scl[:, 96:128] = np.asarray(I["L0_ffn_h2_s"]).reshape(32, 128).T
    scl[:, 128:136] = np.asarray(I["L0_ffn_w3_s"]).reshape(8, 128).T
    return {"w12": np.ascontiguousarray(w12), "h1w": h1w, "h2w": h2w,
            "w3w": w3w, "scl": np.ascontiguousarray(scl)}


def prep_l1_ffn(I):
    W1 = np.asarray(I["L1_ffn_W1"])
    W2 = np.asarray(I["L1_ffn_W2"])
    w1 = np.stack([_tile_kxm(W1[e]) for e in range(NE)])
    w2 = np.stack([_tile_kxm(W2[e]) for e in range(NE)])
    rw = np.asarray(I["L1_ffn_router_W"]).reshape(8, 128, 8)
    rw = np.ascontiguousarray(rw.transpose(1, 0, 2).reshape(128, 64))
    rb = np.asarray(I["L1_ffn_router_b"]).reshape(1, 8)
    lng = np.asarray(I["L1_ffn_ln_g"]).T.reshape(8, 128, 8)
    lng = np.ascontiguousarray(lng.transpose(1, 0, 2).reshape(128, 64))
    lnb = np.asarray(I["L1_ffn_ln_b"]).T.reshape(8, 128, 8)
    lnb = np.ascontiguousarray(lnb.transpose(1, 0, 2).reshape(128, 64))
    b1 = np.asarray(I["L1_ffn_b1"]).reshape(8, 32, 128)
    b1 = np.ascontiguousarray(b1.transpose(2, 0, 1).reshape(128, 256))
    b2 = np.ascontiguousarray(np.asarray(I["L1_ffn_b2"]).astype(np.float32))
    sel = np.zeros((8, 1024), np.float32)
    for e in range(NE):
        sel[e, e * 128:(e + 1) * 128] = 1.0
    return {"w1": np.ascontiguousarray(w1), "w2": np.ascontiguousarray(w2),
            "rw": rw, "rb": np.ascontiguousarray(rb.astype(np.float32)),
            "lng": lng, "lnb": lnb, "b1": b1, "b2": b2, "sel": sel,
            "idf": np.eye(128, dtype=np.float32)}



_NC_CACHE = {}


def _get_nc(key):
    if key not in _NC_CACHE:
        if key == "l0a":
            _NC_CACHE[key] = build_attn(0)
        elif key == "l1a":
            _NC_CACHE[key] = build_attn(1)
        elif key == "l0f":
            _NC_CACHE[key] = build_l0_ffn()
        elif key == "l1f":
            _NC_CACHE[key] = build_l1_ffn()
    return _NC_CACHE[key]


def _run(key, per_core_const, xts):
    nc = _get_nc(key)
    in_maps = []
    for b in range(8):
        m = dict(per_core_const)
        m["xin"] = xts[b]
        in_maps.append(m)
    res = run_bass_kernel_spmd(nc, in_maps, core_ids=list(range(8)))
    return [np.asarray(res.results[b]["xout"]) for b in range(8)]


def _fence(p):
    for nm in ("pe", "ve", "se", "wl", "sl"):
        v = p.cur(nm)
        for eng in ENGS:
            p.w(eng, nm, v)


def build_fused():
    nc = bass.Bass("TRN2", target_bir_lowering=False)
    d = nc.dram_tensor
    xin = d("xin", [DM, S], F32, kind="ExternalInput").ap()
    xout = d("xout", [DM, S], F32, kind="ExternalOutput").ap()
    hb = {
        "wa0": d("wa0", [NPAIR, 3, 8, 128, 128], F32, kind="ExternalInput").ap(),
        "wp0": d("wp0", [8, 8, 128, 128], F32, kind="ExternalInput").ap(),
        "biast0": d("biast0", [128, 32], F32, kind="ExternalInput").ap(),
        "iden": d("iden", [128, 128], BF16, kind="ExternalInput").ap(),
        "w12": d("w12", [FCH, 2, 8, 128, 128], F32, kind="ExternalInput").ap(),
        "h1w": d("h1w", [FCH, FCH, 128, 128], F32, kind="ExternalInput").ap(),
        "h2w": d("h2w", [FCH, FCH, 128, 128], F32, kind="ExternalInput").ap(),
        "w3w": d("w3w", [8, FCH, 128, 128], F32, kind="ExternalInput").ap(),
        "scl": d("scl", [128, 144], F32, kind="ExternalInput").ap(),
        "wa1": d("wa1", [NPAIR, 3, 8, 128, 128], F32, kind="ExternalInput").ap(),
        "wp1": d("wp1", [8, 8, 128, 128], F32, kind="ExternalInput").ap(),
        "biast1": d("biast1", [128, 32], F32, kind="ExternalInput").ap(),
        "cscs": d("cscs", [128, S], BF16, kind="ExternalInput").ap(),
        "scsc": d("scsc", [128, S], BF16, kind="ExternalInput").ap(),
        "mneg": d("mneg", [128, 128], BF16, kind="ExternalInput").ap(),
        "c1m": d("c1m", [128, 128], BF16, kind="ExternalInput").ap(),
        "c2m": d("c2m", [128, 128], BF16, kind="ExternalInput").ap(),
        "w1e": d("w1e", [NE, FCH, 8, 128, 128], F32, kind="ExternalInput").ap(),
        "w2e": d("w2e", [NE, NCH, FCH, 128, 128], F32, kind="ExternalInput").ap(),
        "rw": d("rw", [128, 64], F32, kind="ExternalInput").ap(),
        "rb": d("rb", [1, 8], F32, kind="ExternalInput").ap(),
        "lng": d("lng", [128, 64], F32, kind="ExternalInput").ap(),
        "lnb": d("lnb", [128, 64], F32, kind="ExternalInput").ap(),
        "b1h": d("b1h", [128, 256], F32, kind="ExternalInput").ap(),
        "b2h": d("b2h", [8, 1024], F32, kind="ExternalInput").ap(),
        "selh": d("selh", [8, 1024], BF16, kind="ExternalInput").ap(),
        "idf": d("idf", [128, 128], F32, kind="ExternalInput").ap(),
    }

    es = ExitStack()
    sb = nc.sbuf_tensor
    C = {"nc": nc}
    C["xf32"] = es.enter_context(sb("xf32", [128, NCH * S], F32))
    C["xb16"] = es.enter_context(sb("xb16", [128, NCH * S], BF16))
    C["pA"] = es.enter_context(nc.psum_tensor("pA", [128, 2048], F32))
    C["pB"] = es.enter_context(nc.psum_tensor("pB", [128, 1024], F32))
    C["pT1"] = es.enter_context(nc.psum_tensor("pT1", [128, 1024], BF16))
    C["pT2"] = es.enter_context(nc.psum_tensor("pT2", [128, 1024], BF16))
    C["lnst"] = es.enter_context(sb("lnst", [128, 2 * S], F32))
    C["onesk"] = es.enter_context(sb("onesk", [128, 1], BF16))
    C["onesf"] = es.enter_context(sb("onesf", [1, 128], F32))
    C["epsv"] = es.enter_context(sb("epsv", [1, 1], F32))
    C["iden"] = es.enter_context(sb("idens", [128, 128], BF16))
    C["nmx"] = es.enter_context(sb("nmx", [128, 4], F32))
    C["sums"] = es.enter_context(sb("sums", [128, 8], F32))
    C["pmx"] = es.enter_context(sb("pmx", [128, 8], F32))
    C["stot"] = es.enter_context(sb("stot", [128, 2], F32))
    A16 = es.enter_context(sb("A16", [128, 41216], BF16))
    A32 = es.enter_context(sb("A32", [128, 1368], F32))
    GTS = es.enter_context(sb("GTS", [128, 4096], BF16))
    C["mneg16"] = es.enter_context(sb("mneg16", [128, 128], BF16))
    C["msk1b"] = es.enter_context(sb("msk1b", [128, 128], BF16))
    C["msk2b"] = C["mneg16"]
    C["gtok16"] = C["msk1b"]

    def v16(c0, n):
        return A16[:, c0:c0 + n]

    def v32(c0, n):
        return A32[:, c0:c0 + n]

    LGT = 0
    GSC, BIAS0, BIAS1 = 128, 192, 224
    SCL, B1T, LNG, LNB, RW, RB, BCG = 256, 400, 656, 720, 784, 848, 856

    p = P()
    emit_prelude(p, C, xin)
    for ap_sb, ap_hb in ((C["iden"][:, :], hb["iden"]), (v32(BIAS0, 32), hb["biast0"]),
                         (v32(BIAS1, 32), hb["biast1"]),
                         (C["mneg16"][:, :], hb["mneg"]),
                         (v32(SCL, 144), hb["scl"]), (v32(B1T, 256), hb["b1h"]),
                         (v32(LNG, 64), hb["lng"]), (v32(LNB, 64), hb["lnb"]),
                         (v32(RW, 64), hb["rw"])):
        p.dma("sync", ap_sb, ap_hb, "sl")
    p.dma("sync", A32[0:1, RB:RB + 8], hb["rb"], "sl")
    sl0 = p.cur("sl")

    Ba = dict(C)
    Ba.update(qkvp=v16(0, 3 * S), vtok=v16(6144, 2048), e16=v16(8192, 2 * S),
              attnT=v16(12288, 2 * S), wbuf=v16(16384, 3072),
              wpbuf=v16(19456, 1024), outT=v16(20480, 8 * S),
              sq16=v16(20480, 8 * S), bias=v32(BIAS0, 32),
              tmp32=v16(20480, 1024).bitcast(F32),
              wa_hbm=hb["wa0"], wp_hbm=hb["wp0"], sl_done=sl0, wslots=1)
    p.w("tensor", "sl", sl0)
    emit_attention(p, Ba, 0)
    _fence(p)

    Bf = dict(C)
    Bf.update(h0=v16(0, 32768), h1bc=C["xf32"][:, :].bitcast(BF16),
              sq16=v16(0, NCH * S), tmp32=v16(0, 1024).bitcast(F32),
              a_scr=C["lnst"][:, :].bitcast(BF16)[:, 0:2048],
              wbig=v16(32768, 8192), scl=v32(SCL, 144),
              xsp_hbm=nc.dram_tensor("xsp", [128, NCH * S], F32,
                                     kind="Internal").ap(),
              w12_hbm=hb["w12"], h1_hbm=hb["h1w"], h2_hbm=hb["h2w"],
              w3_hbm=hb["w3w"], sl_done=sl0)
    emit_l0_ffn(p, Bf)
    _fence(p)

    p.dma("sync", v16(36864, 2048), hb["cscs"], "sl")
    p.dma("sync", v16(38912, 2048), hb["scsc"], "sl")
    p.dma("sync", v16(40960, 128), hb["c1m"], "sl")
    p.dma("sync", v16(41088, 128), hb["c2m"], "sl")
    sl1 = p.cur("sl")
    Bb = dict(C)
    Bb.update(qkvp=v16(0, 3 * S), vtok=v16(6144, 2048), e16=v16(8192, 2 * S),
              attnT=v16(12288, 2 * S), wbuf=v16(16384, 3072),
              wpbuf=v16(19456, 1024), outT=v16(20480, 8 * S),
              sq16=v16(20480, 8 * S), bias=v32(BIAS1, 32),
              tmp32=v16(20480, 1024).bitcast(F32),
              cscs=v16(36864, 2048), scsc=v16(38912, 2048),
              c1m=v16(40960, 128), c2m=v16(41088, 128),
              t1s=v16(8192, 2048), t2s=v16(10240, 2048),
              maskneg=C["mneg16"][:, :],
              wa_hbm=hb["wa1"], wp_hbm=hb["wp1"], sl_done=sl1, wslots=1)
    p.w("tensor", "sl", sl1)
    p.w("vector", "sl", sl1)
    p.w("scalar", "sl", sl1)
    emit_attention(p, Bb, 1)
    _fence(p)

    Bm = dict(C)
    Bm.update(ae=v16(0, 32768), sq16=v16(0, 16384),
              tmp32=v16(0, 1024).bitcast(F32),
              wbig=v16(32768, 8192), gT=GTS[0:8, 0:2048],
              sel=GTS[0:8, 2048:3072], b2_sb=GTS[0:8, 3072:4096],
              rw_sb=v32(RW, 64), rb_sb=A32[0:1, RB:RB + 8],
              gtok=v32(LGT, 128), lgt=v32(LGT, 128),
              gsc=v32(GSC, 64), msk1=C["msk1b"][:, :], msk2=C["msk2b"][:, :],
              lng_sb=v32(LNG, 64), lnb_sb=v32(LNB, 64), b1_sb=v32(B1T, 256),
              bcG=v32(BCG, 512).bitcast(BF16), w1_hbm=hb["w1e"],
              w2_hbm=hb["w2e"], sl_done=sl0)
    p.dma("sync", Bm["sel"], hb["selh"], "sl")
    p.dma("gpsimd", Bm["b2_sb"], hb["b2h"], "sl")
    Bm["sl_done"] = p.cur("sl")
    p.w("vector", "sl", Bm["sl_done"])
    p.w("scalar", "sl", Bm["sl_done"])
    emit_l1_ffn(p, Bm)
    emit_store(p, C, xout)
    p.emit(nc, es)
    es.close()
    return nc


def prep_fused(I):
    a0 = prep_l0_attn(I)
    f0 = prep_l0_ffn(I)
    a1 = prep_l1_attn(I)
    f1 = prep_l1_ffn(I)
    sel16 = np.zeros((8, 1024), BF)
    for e in range(NE):
        sel16[e, e * 128:(e + 1) * 128] = 1.0
    return {
        "wa0": a0["wa"], "wp0": a0["wp"], "biast0": a0["biast"],
        "iden": a0["iden"],
        "w12": f0["w12"], "h1w": f0["h1w"], "h2w": f0["h2w"],
        "w3w": f0["w3w"], "scl": f0["scl"],
        "wa1": a1["wa"], "wp1": a1["wp"], "biast1": a1["biast"],
        "cscs": a1["cscs"], "scsc": a1["scsc"], "mneg": a1["mneg"],
        "c1m": a1["c1m"], "c2m": a1["c2m"],
        "w1e": f1["w1"], "w2e": f1["w2"], "rw": f1["rw"], "rb": f1["rb"],
        "lng": f1["lng"], "lnb": f1["lnb"], "b1h": f1["b1"], "b2h": f1["b2"],
        "selh": sel16, "idf": f1["idf"],
    }


def kernel(**I):
    x = np.asarray(I["L0_attn_x"])
    xts = [np.ascontiguousarray(x[b].T) for b in range(8)]
    if "fused" not in _NC_CACHE:
        _NC_CACHE["fused"] = build_fused()
    nc = _NC_CACHE["fused"]
    const = prep_fused(I)
    in_maps = []
    for b in range(8):
        m = dict(const)
        m["xin"] = xts[b]
        in_maps.append(m)
    res = run_bass_kernel_spmd(nc, in_maps, core_ids=list(range(8)))
    out = np.stack([np.asarray(res.results[b]["xout"]).T for b in range(8)])
    return np.ascontiguousarray(out.astype(np.float32))


def kernel_unfused(**I):
    x = np.asarray(I["L0_attn_x"])
    xts = [np.ascontiguousarray(x[b].T) for b in range(8)]

    c0a = prep_l0_attn(I)
    xts = _run("l0a", c0a, xts)
    c0f = prep_l0_ffn(I)
    xts = _run("l0f", c0f, xts)
    c1a = prep_l1_attn(I)
    xts = _run("l1a", c1a, xts)
    c1f = prep_l1_ffn(I)
    xts = _run("l1f", c1f, xts)

    out = np.stack([xts[b].T for b in range(8)])
    return np.ascontiguousarray(out.astype(np.float32))


# version 14
# speedup vs baseline: 1.0198x; 1.0198x over previous
import numpy as np
import ml_dtypes
from contextlib import ExitStack

import concourse.bass as bass
import concourse.mybir as mybir
from concourse.bass_utils import run_bass_kernel_spmd

F32 = mybir.dt.float32
BF16 = mybir.dt.bfloat16
AF = mybir.ActivationFunctionType
OP = mybir.AluOpType
AX = mybir.AxisListType.X
BF = ml_dtypes.bfloat16

S = 2048
DM = 1024
NCH = DM // 128
H = 16
DK = 64
FFD = 4096
FCH = FFD // 128
NE = 8
NPAIR = 8
EPS = 1e-5
SCALE_QK = 0.125

ENGS = ("tensor", "vector", "scalar", "gpsimd", "sync")


class P:

    def __init__(self):
        self.ops = {e: [] for e in ENGS}
        self.n = {}
        self.waits = []

    def sem(self, *names):
        for nm in names:
            self.n.setdefault(nm, 0)

    def cur(self, nm):
        return self.n[nm]

    def w(self, eng, nm, val):
        if val <= 0:
            return
        self.waits.append((nm, val))
        self.ops[eng].append(("w", nm, val))

    def o(self, eng, f, inc=None):
        if inc is not None and not isinstance(inc, list):
            inc = [inc]
        if inc:
            for nm, amt in inc:
                self.n[nm] += amt
        self.ops[eng].append(("o", f, inc))

    def dma(self, eng, out, in_, nm="wl"):
        self.o(eng, lambda e, o=out, i=in_: e.dma_start(out=o, in_=i), inc=(nm, 16))
        return self.n[nm]

    def emit(self, nc, es):
        for nm, val in self.waits:
            assert val <= self.n[nm], f"wait {nm}>={val} exceeds total {self.n[nm]}"
        handles = {nm: es.enter_context(nc.semaphore(nm)) for nm in self.n}
        block = es.enter_context(nc.Block())

        def runner(ops):
            def run(e):
                for item in ops:
                    if item[0] == "w":
                        e.wait_ge(handles[item[1]], item[2])
                    else:
                        _, f, inc = item
                        ins = f(e)
                        if inc:
                            for nm, amt in inc:
                                ins.then_inc(handles[nm], amt)
            return run

        for eng in ENGS:
            if self.ops[eng]:
                getattr(block, eng)(runner(self.ops[eng]))


def fm(buf, c, t0=0, t1=S):
    return buf[:, c * S + t0: c * S + t1]


def emit_ln_stats(p, B, src=None, ob=None):
    xf32, xb16 = B["xf32"], B["xb16"]
    src = src if src is not None else xf32
    ob = ob if ob is not None else xb16
    pA, sq16 = B["pA"], B["sq16"]
    onesk = B["onesk"]
    st = B["lnst"]

    p.w("scalar", "ve", p.cur("ve"))
    p.w("scalar", "pe", p.cur("pe"))
    for c in range(NCH):
        p.o("scalar", lambda e, c=c: e.activation(fm(ob, c), fm(src, c), AF.Copy),
            inc=("se", 1))
    se_cast = p.cur("se")
    for c in range(NCH):
        p.o("scalar", lambda e, c=c: e.activation(
            sq16[:, c * S:(c + 1) * S], fm(src, c), AF.Square), inc=("se", 1))
    se_sq = p.cur("se")

    p.w("tensor", "se", se_cast)
    for g in range(4):
        for c in range(NCH):
            p.o("tensor", lambda e, c=c, g=g: e.matmul(
                pA[0:1, g * 512:(g + 1) * 512], onesk[:, :],
                fm(ob, c, g * 512, (g + 1) * 512),
                start=(c == 0), stop=(c == NCH - 1)),
                inc=("pe", 1) if c == NCH - 1 else None)
    pe_sum = p.cur("pe")
    stA = st[0:1, 0:S]
    stB = st[0:1, S:2 * S]
    p.w("vector", "pe", pe_sum)
    p.o("vector", lambda e: e.tensor_scalar(stB, pA[0:1, :], 1.0 / DM, None,
                                            OP.mult), inc=("ve", 1))
    ve_mean = p.cur("ve")

    p.w("tensor", "ve", ve_mean)
    p.w("tensor", "se", se_sq)
    for g in range(4):
        for c in range(NCH):
            p.o("tensor", lambda e, c=c, g=g: e.matmul(
                pA[0:1, g * 512:(g + 1) * 512], onesk[:, :],
                sq16[:, c * S + g * 512: c * S + (g + 1) * 512],
                start=(c == 0), stop=(c == NCH - 1)),
                inc=("pe", 1) if c == NCH - 1 else None)
    pe_ssq = p.cur("pe")

    p.w("vector", "pe", pe_ssq)
    p.w("vector", "ve", ve_mean)
    p.o("vector", lambda e: e.tensor_tensor(stA, stB, stB, OP.mult),
        inc=("ve", 1))
    p.w("vector", "ve", p.cur("ve"))
    p.o("vector", lambda e: e.scalar_tensor_tensor(
        stA, pA[0:1, :], 1.0 / DM, stA, OP.mult, OP.subtract), inc=("ve", 1))
    ve_var = p.cur("ve")
    p.w("scalar", "ve", ve_var)
    p.o("scalar", lambda e: e.activation(pA[0:1, :], stA, AF.Ln,
                                         bias=B["epsv"][:, :]), inc=("se", 1))
    p.w("scalar", "se", p.cur("se"))
    p.o("scalar", lambda e: e.activation(stA, pA[0:1, :], AF.Exp,
                                         scale=-0.5), inc=("se", 1))
    se_A = p.cur("se")
    p.w("vector", "se", se_A)
    p.o("vector", lambda e: e.scalar_tensor_tensor(
        stB, stB, -1.0, stA, OP.mult, OP.mult), inc=("ve", 1))
    return p.cur("ve")


def emit_ln_apply(p, B, ve_B, mode="stream", src=None, ob=None, g0=0, g1=4):
    xf32, xb16 = B["xf32"], B["xb16"]
    src = src if src is not None else xf32
    ob = ob if ob is not None else xb16
    pA = B["pA"]
    onesf = B["onesf"]
    st = B["lnst"]
    tmp32 = B["tmp32"]
    for g in range(g0, g1):
        p.w("tensor", "ve", ve_B if g == g0 else p.cur("ve"))
        p.o("tensor", lambda e, g=g: e.matmul(
            pA[:, 0:512], onesf[:, :], st[0:1, g * 512:(g + 1) * 512],
            start=True, stop=True))
        p.o("tensor", lambda e, g=g: e.matmul(
            pA[:, 512:1024], onesf[:, :], st[0:1, S + g * 512:S + (g + 1) * 512],
            start=True, stop=True), inc=("pe", 1))
        pe_bc = p.cur("pe")
        p.w("vector", "pe", pe_bc)
        for c in range(NCH):
            p.o("vector", lambda e, c=c, g=g: e.tensor_tensor(
                tmp32[:, :], fm(src, c, g * 512, (g + 1) * 512), pA[:, 0:512],
                OP.mult), inc=("ve", 1))
            p.w("vector", "ve", p.cur("ve"))
            if mode == "b16":
                p.o("vector", lambda e, c=c, g=g: e.tensor_tensor(
                    fm(ob, c, g * 512, (g + 1) * 512), tmp32[:, :],
                    pA[:, 512:1024], OP.add), inc=("ve", 1))
            else:
                p.o("vector", lambda e, c=c, g=g: e.tensor_tensor(
                    fm(xf32, c, g * 512, (g + 1) * 512), tmp32[:, :],
                    pA[:, 512:1024], OP.add), inc=("ve", 1))
                if mode == "stream":
                    p.w("scalar", "ve", p.cur("ve"))
                    p.o("scalar", lambda e, c=c, g=g: e.activation(
                        fm(ob, c, g * 512, (g + 1) * 512),
                        fm(xf32, c, g * 512, (g + 1) * 512), AF.Copy),
                        inc=("se", 1))


def emit_ln(p, B, mode="stream", src=None, ob=None):
    ve_B = emit_ln_stats(p, B, src=src, ob=ob)
    emit_ln_apply(p, B, ve_B, mode=mode, src=src, ob=ob)


def emit_attention(p, B, layer):
    l1 = (layer == 1)
    xf32, xb16 = B["xf32"], B["xb16"]
    pA, pB, pT1, pT2 = B["pA"], B["pB"], B["pT1"], B["pT2"]
    qkvp, vtok = B["qkvp"], B["vtok"]
    e16, attnT, outT = B["e16"], B["attnT"], B["outT"]
    nmx, sums, pmx, stot = B["nmx"], B["sums"], B["pmx"], B["stot"]
    iden = B["iden"]
    wbuf, wpb = B["wbuf"], B["wpbuf"]
    bias = B["bias"]
    whbm, wp_hbm = B["wa_hbm"], B["wp_hbm"]
    if l1:
        cscs, scsc = B["cscs"], B["scsc"]
        maskneg = B["maskneg"]
        t1s, t2s = B["t1s"], B["t2s"]

    p.w("vector", "sl", B["sl_done"])
    p.o("vector", lambda e: e.tensor_scalar(bias[:, 0:8], bias[:, 0:8], SCALE_QK,
                                            None, OP.mult), inc=("ve", 1))
    p.w("scalar", "ve", p.cur("ve"))

    pair_mmdone = [0] * NPAIR
    pa_se = [0, 0, 0, 0]
    pa_ve = [0, 0, 0, 0]
    ptA_free = ("se", 0)
    ptB_free = ("se", 0)
    av_free = [0, 0]
    e16_free = [0, 0]
    e16_norm = [0, 0]
    ti = 0
    pend = None
    pend2 = None

    def emit_transp(t):
        nonlocal ptA_free, ptB_free
        pr, nk = t["pr"], t["nk"]
        eb = e16[:, pr * S:(pr + 1) * S]
        p.w("tensor", "ve", t["norm_ve"])
        t["tp_pe"] = []
        for half in range((nk + 7) // 8):
            j0, j1 = half * 8, min(nk, half * 8 + 8)
            pt = pT1 if half == 0 else pT2
            fr = ptA_free if half == 0 else ptB_free
            p.w("tensor", fr[0], fr[1])
            for j in range(j0, j1):
                p.o("tensor", lambda e, j=j, j0=j0, pt=pt, eb=eb:
                    e.transpose(pt[:, (j - j0) * 128:(j - j0 + 1) * 128],
                                eb[:, j * 128:(j + 1) * 128], iden[:, :]),
                    inc=("pe", 1) if j == j1 - 1 else None)
            t["tp_pe"].append(p.cur("pe"))
        e16_free[pr] = p.cur("pe")

    def emit_copies(t):
        nonlocal ptA_free, ptB_free
        pr, nk = t["pr"], t["nk"]
        at = attnT[:, pr * S:(pr + 1) * S]
        for half in range((nk + 7) // 8):
            j0, j1 = half * 8, min(nk, half * 8 + 8)
            pt = pT1 if half == 0 else pT2
            p.w("scalar", "pe", t["tp_pe"][half])
            p.o("scalar", lambda e, j0=j0, j1=j1, pt=pt, at=at: e.activation(
                at[:, j0 * 128: j1 * 128], pt[:, 0:(j1 - j0) * 128], AF.Copy),
                inc=("se", 1))
            if half == 0:
                ptA_free = ("se", p.cur("se"))
            else:
                ptB_free = ("se", p.cur("se"))
        t["cp_se"] = p.cur("se")

    def emit_av(t):
        pr, nk, hg, qt = t["pr"], t["nk"], t["hg"], t["qt"]
        at = attnT[:, pr * S:(pr + 1) * S]
        avb = pr
        p.w("tensor", "se", max(t["cp_se"], av_free[avb]))
        for kc in range(nk):
            p.o("tensor", lambda e, kc=kc, avb=avb, nk=nk, at=at, t=t:
                e.matmul(pB[0:64, avb * 512: avb * 512 + 128],
                         vtok[:, t["vbase"] + kc * 64: t["vbase"] + (kc + 1) * 64],
                         at[:, kc * 128:(kc + 1) * 128],
                         start=(kc == 0), stop=(kc == nk - 1)),
                inc=("pe", 1) if kc == nk - 1 else None)
        p.w("scalar", "pe", p.cur("pe"))
        p.o("scalar", lambda e, hg=hg, qt=qt, avb=avb: e.activation(
            outT[(hg % 2) * 64:(hg % 2) * 64 + 64,
                 (hg // 2) * S + qt * 128:(hg // 2) * S + (qt + 1) * 128],
            pB[0:64, avb * 512: avb * 512 + 128], AF.Copy), inc=("se", 1))
        av_free[avb] = p.cur("se")

    def drain():
        nonlocal pend, pend2
        if pend is not None:
            emit_transp(pend)
        if pend2 is not None:
            emit_av(pend2)
        if pend is not None:
            emit_copies(pend)
            emit_av(pend)
        pend = pend2 = None

    nws = B.get("wslots", 1)
    for pp in range(NPAIR):
        if pp >= nws:
            p.w("gpsimd", "pe", pair_mmdone[pp - nws])
        wl = p.dma("gpsimd", wbuf[:, (pp % nws) * 3072:(pp % nws) * 3072 + 3072]
                   .rearrange("p (t k m) -> p t k m", t=3, k=8),
                   whbm[pp].rearrange("t k p m -> p t k m"), "wl")

        p.w("tensor", "wl", wl)
        for t3 in range(3):
            for gp in range(2):
                b0, b1 = 2 * gp, 2 * gp + 1
                p.w("tensor", "se", max(pa_se[b0], pa_se[b1]))
                p.w("tensor", "ve", max(pa_ve[b0], pa_ve[b1]))
                for kc in range(8):
                    for gi in range(2):
                        g = 2 * gp + gi
                        p.o("tensor", lambda e, pp=pp, t3=t3, g=g, kc=kc:
                            e.matmul(pA[:, g * 512:(g + 1) * 512],
                                     wbuf[:, (pp % nws) * 3072 + (t3 * 8 + kc) * 128:
                                          (pp % nws) * 3072 + (t3 * 8 + kc) * 128 + 128],
                                     fm(xb16, kc, g * 512, (g + 1) * 512),
                                     start=(kc == 0), stop=(kc == 7)),
                            inc=("pe", 1) if (kc == 7 and gi == 1) else None)
                pe_mm = p.cur("pe")
                p.w("scalar", "pe", pe_mm)
                sc = SCALE_QK if t3 == 0 else 1.0
                bcol = t3 * 8 + pp
                for g in (b0, b1):
                    p.o("scalar", lambda e, t3=t3, g=g, sc=sc, bcol=bcol:
                        e.activation(qkvp[:, t3 * S + g * 512: t3 * S + (g + 1) * 512],
                                     pA[:, g * 512:(g + 1) * 512], AF.Identity,
                                     bias=bias[:, bcol:bcol + 1], scale=sc),
                        inc=("se", 1))
                    pa_se[g] = p.cur("se")
        pair_mmdone[pp] = p.cur("pe")
        se_qkv = p.cur("se")

        drain()

        if l1:
            c1m, c2m = B["c1m"], B["c2m"]
            for t3 in range(2):
                off = t3 * S
                p.w("vector", "se", p.cur("se"))
                p.w("vector", "ve", p.cur("ve"))
                p.o("vector", lambda e, off=off: e.tensor_tensor(
                    t1s[:, :], qkvp[:, off:off + S], cscs[:, :], OP.mult),
                    inc=("ve", 1))
                p.o("vector", lambda e, off=off: e.tensor_tensor(
                    t2s[:, :], qkvp[:, off:off + S], scsc[:, :], OP.mult),
                    inc=("ve", 1))
                ve_t12 = p.cur("ve")
                p.w("tensor", "ve", ve_t12)
                p.w("tensor", "se", max(pa_se))
                p.w("tensor", "ve", max(pa_ve))
                for g in range(4):
                    p.o("tensor", lambda e, g=g: e.matmul(
                        pA[:, g * 512:(g + 1) * 512], c1m[:, :],
                        t1s[:, g * 512:(g + 1) * 512], start=True, stop=False))
                    p.o("tensor", lambda e, g=g: e.matmul(
                        pA[:, g * 512:(g + 1) * 512], c2m[:, :],
                        t2s[:, g * 512:(g + 1) * 512], start=False, stop=True),
                        inc=("pe", 1) if g == 3 else None)
                p.w("scalar", "pe", p.cur("pe"))
                for g in range(4):
                    p.o("scalar", lambda e, off=off, g=g:
                        e.activation(qkvp[:, off + g * 512: off + (g + 1) * 512],
                                     pA[:, g * 512:(g + 1) * 512],
                                     AF.Copy), inc=("se", 1))
                    pa_se[g] = p.cur("se")

        p.w("tensor", "se", se_qkv)
        for hh in range(2):
            for half in range(2):
                pt = pT1 if half == 0 else pT2
                fr = ptA_free if half == 0 else ptB_free
                p.w("tensor", fr[0], fr[1])
                for j in range(8):
                    kc = half * 8 + j
                    p.o("tensor", lambda e, hh=hh, kc=kc, j=j, pt=pt:
                        e.transpose(pt[:, j * 64:(j + 1) * 64],
                                    qkvp[hh * 64:(hh + 1) * 64,
                                         2 * S + kc * 128: 2 * S + (kc + 1) * 128],
                                    iden[hh * 64:(hh + 1) * 64,
                                         hh * 64:(hh + 1) * 64]),
                        inc=("pe", 1) if j == 7 else None)
                p.w("scalar", "pe", p.cur("pe"))
                p.o("scalar", lambda e, hh=hh, half=half, pt=pt: e.activation(
                    vtok[:, hh * 1024 + half * 512: hh * 1024 + (half + 1) * 512],
                    pt[:, 0:512], AF.Copy), inc=("se", 1))
                if half == 0:
                    ptA_free = ("se", p.cur("se"))
                else:
                    ptB_free = ("se", p.cur("se"))

        for hh in range(2):
            hg = 2 * pp + hh
            for qt in range(16):
                nk = (qt + 1) if l1 else 16
                ng = (nk * 128 + 511) // 512
                pr = ti % 2
                ti += 1
                eb = e16[:, pr * S:(pr + 1) * S]
                this = dict(pr=pr, hh=hh, qt=qt, nk=nk, hg=hg, norm_ve=None,
                            vbase=hh * 1024)
                p.w("tensor", "se", max(pa_se[g] for g in range(ng)))
                p.w("tensor", "ve", max(pa_ve[g] for g in range(ng)))
                pe_sc = []
                for g in range(ng):
                    n0, n1 = g * 512, min(nk * 128, (g + 1) * 512)
                    p.o("tensor", lambda e, hh=hh, qt=qt, n0=n0, n1=n1:
                        e.matmul(pA[:, n0:n1],
                                 qkvp[hh * 64:(hh + 1) * 64,
                                      qt * 128:(qt + 1) * 128],
                                 qkvp[hh * 64:(hh + 1) * 64, S + n0: S + n1],
                                 start=True, stop=True), inc=("pe", 1))
                    pe_sc.append(p.cur("pe"))
                pe_sc_all = p.cur("pe")
                if pend is not None:
                    emit_transp(pend)
                if pend2 is not None:
                    emit_av(pend2)
                gd = (qt * 128) // 512 if l1 else -1
                for g in range(ng):
                    n0, n1 = g * 512, min(nk * 128, (g + 1) * 512)
                    p.w("vector", "pe", pe_sc[g])
                    if g == gd:
                        p.o("vector", lambda e, qt=qt: e.tensor_tensor(
                            pA[:, qt * 128:(qt + 1) * 128],
                            pA[:, qt * 128:(qt + 1) * 128], maskneg[:, :],
                            OP.add), inc=("ve", 1))
                        p.w("vector", "ve", p.cur("ve"))
                    p.o("vector", lambda e, g=g, n0=n0, n1=n1, pr=pr:
                        e.reduce_max(pmx[:, pr * 4 + g:pr * 4 + g + 1],
                                     pA[:, n0:n1], axis=AX), inc=("ve", 1))
                p.w("vector", "ve", p.cur("ve"))
                p.o("vector", lambda e, pr=pr, ng=ng: e.reduce_max(
                    nmx[:, pr * 2:pr * 2 + 1], pmx[:, pr * 4:pr * 4 + ng],
                    axis=AX, negate=True), inc=("ve", 1))
                ve_max = p.cur("ve")
                p.w("scalar", "ve", max(ve_max, e16_norm[pr]))
                p.w("scalar", "pe", e16_free[pr])
                p.o("scalar", lambda e, nk=nk, pr=pr, eb=eb:
                    e.activation(eb[:, 0:nk * 128], pA[:, 0:nk * 128], AF.Exp,
                                 bias=nmx[:, pr * 2:pr * 2 + 1],
                                 accum_out=sums[:, pr * 4:pr * 4 + 1]),
                    inc=("se", 1))
                se_exp = p.cur("se")
                for g in range(ng):
                    pa_se[g] = se_exp
                    pa_ve[g] = 0
                if pend is not None:
                    emit_copies(pend)
                p.w("vector", "se", se_exp)
                p.o("vector", lambda e, pr=pr: e.reciprocal(
                    nmx[:, pr * 2 + 1:pr * 2 + 2], sums[:, pr * 4:pr * 4 + 1]),
                    inc=("ve", 1))
                p.w("vector", "ve", p.cur("ve"))
                p.o("vector", lambda e, nk=nk, pr=pr, eb=eb: e.tensor_scalar(
                    eb[:, 0:nk * 128], eb[:, 0:nk * 128],
                    nmx[:, pr * 2 + 1:pr * 2 + 2], None, OP.mult), inc=("ve", 1))
                this["norm_ve"] = p.cur("ve")
                e16_norm[pr] = this["norm_ve"]
                pend2 = pend
                pend = this
    drain()

    se_attn_all = p.cur("se")
    ve_attn_all = p.cur("ve")
    proj_pe = [0] * 8
    for m in range(8):
        if m >= nws:
            p.w("gpsimd", "pe", proj_pe[m - nws])
        wl = p.dma("gpsimd", wpb[:, (m % nws) * 1024:(m % nws) * 1024 + 1024]
                   .rearrange("p (k c) -> p k c", k=8),
                   wp_hbm[m].rearrange("k p c -> p k c"), "wl")
        p.w("tensor", "wl", wl)
        if m == 0:
            p.w("tensor", "se", se_attn_all)
            p.w("tensor", "ve", ve_attn_all)
        for gp in range(2):
            b0, b1 = 2 * gp, 2 * gp + 1
            p.w("tensor", "se", max(pa_se[b0], pa_se[b1]))
            p.w("tensor", "ve", max(pa_ve[b0], pa_ve[b1]))
            for kc in range(8):
                for gi in range(2):
                    g = 2 * gp + gi
                    p.o("tensor", lambda e, m=m, g=g, kc=kc:
                        e.matmul(pA[:, g * 512:(g + 1) * 512],
                                 wpb[:, (m % nws) * 1024 + kc * 128:
                                     (m % nws) * 1024 + (kc + 1) * 128],
                                 outT[:, kc * S + g * 512: kc * S + (g + 1) * 512],
                                 start=(kc == 0), stop=(kc == 7)),
                        inc=("pe", 1) if (kc == 7 and gi == 1) else None)
            pe_mm = p.cur("pe")
            p.w("vector", "pe", pe_mm)
            for g in (b0, b1):
                p.o("vector", lambda e, m=m, g=g: e.scalar_tensor_tensor(
                    fm(xf32, m, g * 512, (g + 1) * 512),
                    pA[:, g * 512:(g + 1) * 512], bias[:, 24 + m:25 + m],
                    fm(xf32, m, g * 512, (g + 1) * 512), OP.add, OP.add),
                    inc=("ve", 1))
                pa_ve[g] = p.cur("ve")
        proj_pe[m] = p.cur("pe")
    p.w("tensor", "ve", p.cur("ve"))
    emit_ln(p, B, "stream")


def emit_l0_ffn(p, B):
    xf32, xb16 = B["xf32"], B["xb16"]
    pA = B["pA"]
    h0, h1 = B["h0"], B["h1bc"]
    a_scr = B["a_scr"]
    wbig = B["wbig"]
    scl = B["scl"]
    xsp = B["xsp_hbm"]
    w12_hbm, h1_hbm, h2_hbm, w3_hbm = (B["w12_hbm"], B["h1_hbm"], B["h2_hbm"],
                                       B["w3_hbm"])
    T = 1024
    ev = {}
    dma_hist = []

    def hq(buf, c):
        return buf[:, c * T:(c + 1) * T]

    for half in range(2):
        t0 = half * T
        p.w("sync", "ve", p.cur("ve"))
        p.w("sync", "se", p.cur("se"))
        sp_done = p.dma("sync", xsp, xf32[:, :], "sl")

        last_stt_ve = [0, 0]
        for m in range(FCH):
            if len(dma_hist) >= 2:
                p.w("gpsimd", "pe", dma_hist[-2])
            wl = p.dma("gpsimd", wbig[:, (m % 2) * 4096:(m % 2) * 4096 + 2048]
                       .rearrange("p (w k c) -> p w k c", w=2, k=8),
                       w12_hbm[m].rearrange("w k p c -> p w k c"), "wl")
            p.w("tensor", "wl", wl)
            b0 = (m % 2) * 2
            p.w("tensor", "se", max(ev.get(("s", b0), 0), ev.get(("s", b0 + 1), 0)))
            p.w("tensor", "ve", max(ev.get(("v", b0), 0), ev.get(("v", b0 + 1), 0)))
            for kc in range(8):
                for gi in range(2):
                    p.o("tensor", lambda e, m=m, kc=kc, b0=b0, gi=gi, t0=t0:
                        e.matmul(pA[:, (b0 + gi) * 512:(b0 + gi + 1) * 512],
                                 wbig[:, (m % 2) * 4096 + kc * 128:
                                      (m % 2) * 4096 + (kc + 1) * 128],
                                 fm(xb16, kc, t0 + gi * 512, t0 + (gi + 1) * 512),
                                 start=(kc == 0), stop=(kc == 7)),
                        inc=("pe", 1) if (kc == 7 and gi == 1) else None)
            p.w("scalar", "pe", p.cur("pe"))
            p.w("scalar", "ve", last_stt_ve[m % 2])
            for gi in range(2):
                p.o("scalar", lambda e, m=m, b0=b0, gi=gi: e.activation(
                    a_scr[:, (m % 2) * 1024 + gi * 512:(m % 2) * 1024 + (gi + 1) * 512],
                    pA[:, (b0 + gi) * 512:(b0 + gi + 1) * 512], AF.Silu,
                    scale=scl[:, m:m + 1]), inc=("se", 1))
            se_a = p.cur("se")
            ev[("s", b0)] = ev[("s", b0 + 1)] = se_a
            p.w("tensor", "se", se_a)
            for kc in range(8):
                for gi in range(2):
                    p.o("tensor", lambda e, m=m, kc=kc, b0=b0, gi=gi, t0=t0:
                        e.matmul(pA[:, (b0 + gi) * 512:(b0 + gi + 1) * 512],
                                 wbig[:, (m % 2) * 4096 + 1024 + kc * 128:
                                      (m % 2) * 4096 + 1024 + (kc + 1) * 128],
                                 fm(xb16, kc, t0 + gi * 512, t0 + (gi + 1) * 512),
                                 start=(kc == 0), stop=(kc == 7)),
                        inc=("pe", 1) if (kc == 7 and gi == 1) else None)
            pe_w2 = p.cur("pe")
            dma_hist.append(pe_w2)
            p.w("vector", "pe", pe_w2)
            p.w("vector", "se", se_a)
            for gi in range(2):
                p.o("vector", lambda e, m=m, b0=b0, gi=gi: e.scalar_tensor_tensor(
                    h0[:, m * T + gi * 512: m * T + (gi + 1) * 512],
                    pA[:, (b0 + gi) * 512:(b0 + gi + 1) * 512],
                    scl[:, 32 + m:33 + m],
                    a_scr[:, (m % 2) * 1024 + gi * 512:(m % 2) * 1024 + (gi + 1) * 512],
                    OP.mult, OP.mult), inc=("ve", 1))
            ve_h0 = p.cur("ve")
            ev[("v", b0)] = ev[("v", b0 + 1)] = ve_h0
            last_stt_ve[m % 2] = ve_h0

        for phase, (hbm, srcb, dstb, scol) in enumerate(
                ((h1_hbm, h0, h1, 64), (h2_hbm, h1, h0, 96))):
            src_cnt = p.cur("ve") if phase == 0 else p.cur("se")
            src_eng = "ve" if phase == 0 else "se"
            for m in range(FCH):
                if len(dma_hist) >= 2:
                    p.w("gpsimd", "pe", dma_hist[-2])
                wl = p.dma("gpsimd",
                           wbig[:, (m % 2) * 4096:(m % 2) * 4096 + 4096]
                           .rearrange("p (k c) -> p k c", k=32),
                           hbm[m].rearrange("k p c -> p k c"), "wl")
                p.w("tensor", "wl", wl)
                b0 = (m % 2) * 2
                p.w("tensor", "se", max(ev.get(("s", b0), 0), ev.get(("s", b0 + 1), 0)))
                p.w("tensor", "ve", max(ev.get(("v", b0), 0), ev.get(("v", b0 + 1), 0)))
                if m == 0:
                    p.w("tensor", src_eng, src_cnt)
                for kc in range(FCH):
                    for gi in range(2):
                        p.o("tensor", lambda e, m=m, kc=kc, b0=b0, gi=gi, srcb=srcb:
                            e.matmul(pA[:, (b0 + gi) * 512:(b0 + gi + 1) * 512],
                                     wbig[:, (m % 2) * 4096 + kc * 128:
                                          (m % 2) * 4096 + (kc + 1) * 128],
                                     srcb[:, kc * T + gi * 512: kc * T + (gi + 1) * 512],
                                     start=(kc == 0), stop=(kc == FCH - 1)),
                            inc=("pe", 1) if (kc == FCH - 1 and gi == 1) else None)
                pe_mm = p.cur("pe")
                dma_hist.append(pe_mm)
                p.w("scalar", "pe", pe_mm)
                if phase == 0 and m == 0:
                    p.w("scalar", "sl", sp_done)
                for gi in range(2):
                    p.o("scalar", lambda e, m=m, b0=b0, gi=gi, dstb=dstb, scol=scol:
                        e.activation(dstb[:, m * T + gi * 512: m * T + (gi + 1) * 512],
                                     pA[:, (b0 + gi) * 512:(b0 + gi + 1) * 512],
                                     AF.Gelu, scale=scl[:, scol + m:scol + m + 1]),
                        inc=("se", 1))
                ev[("s", b0)] = ev[("s", b0 + 1)] = p.cur("se")

        p.w("sync", "pe", p.cur("pe"))
        rs_done = p.dma("sync", xf32[:, :], xsp, "sl")

        se_h2 = p.cur("se")
        for m in range(8):
            if len(dma_hist) >= 2:
                p.w("gpsimd", "pe", dma_hist[-2])
            wl = p.dma("gpsimd", wbig[:, (m % 2) * 4096:(m % 2) * 4096 + 4096]
                       .rearrange("p (k c) -> p k c", k=32),
                       w3_hbm[m].rearrange("k p c -> p k c"), "wl")
            p.w("tensor", "wl", wl)
            b0 = (m % 2) * 2
            p.w("tensor", "se", max(ev.get(("s", b0), 0), ev.get(("s", b0 + 1), 0)))
            p.w("tensor", "ve", max(ev.get(("v", b0), 0), ev.get(("v", b0 + 1), 0)))
            if m == 0:
                p.w("tensor", "se", se_h2)
            for kc in range(FCH):
                for gi in range(2):
                    p.o("tensor", lambda e, m=m, kc=kc, b0=b0, gi=gi:
                        e.matmul(pA[:, (b0 + gi) * 512:(b0 + gi + 1) * 512],
                                 wbig[:, (m % 2) * 4096 + kc * 128:
                                      (m % 2) * 4096 + (kc + 1) * 128],
                                 h0[:, kc * T + gi * 512: kc * T + (gi + 1) * 512],
                                 start=(kc == 0), stop=(kc == FCH - 1)),
                        inc=("pe", 1) if (kc == FCH - 1 and gi == 1) else None)
            pe_mm = p.cur("pe")
            dma_hist.append(pe_mm)
            p.w("vector", "pe", pe_mm)
            if m == 0:
                p.w("vector", "sl", rs_done)
            for gi in range(2):
                p.o("vector", lambda e, m=m, b0=b0, gi=gi, t0=t0:
                    e.scalar_tensor_tensor(
                        fm(xf32, m, t0 + gi * 512, t0 + (gi + 1) * 512),
                        pA[:, (b0 + gi) * 512:(b0 + gi + 1) * 512],
                        scl[:, 128 + m:129 + m],
                        fm(xf32, m, t0 + gi * 512, t0 + (gi + 1) * 512),
                        OP.mult, OP.add), inc=("ve", 1))
            ev[("v", b0)] = ev[("v", b0 + 1)] = p.cur("ve")

    p.w("tensor", "ve", p.cur("ve"))
    emit_ln(p, B, "stream")


def emit_l1_ffn(p, B):
    xf32, xb16 = B["xf32"], B["xb16"]
    pA, pB = B["pA"], B["pB"]
    rW, rb = B["rw_sb"], B["rb_sb"]
    onesf = B["onesf"]
    gtok, gT = B["gtok"], B["gT"]
    lgt, gsc = B["lgt"], B["gsc"]
    msk1, msk2 = B["msk1"], B["msk2"]
    lng, lnb = B["lng_sb"], B["lnb_sb"]
    b1t, b2 = B["b1_sb"], B["b2_sb"]
    sel = B["sel"]
    ae = B["ae"]
    bcG = B["bcG"]
    tmp32 = B["tmp32"]
    w1_hbm, w2_hbm = B["w1_hbm"], B["w2_hbm"]
    wbig = B["wbig"]
    T = 1024

    ev_rt = {}
    p.w("tensor", "sl", B["sl_done"])
    for tt in range(16):
        bank = tt % 4
        p.w("tensor", "ve", ev_rt.get(bank, 0))
        p.o("tensor", lambda e, bank=bank: e.matmul(
            pA[:, bank * 512: bank * 512 + 8], onesf[:, :], rb[:, :],
            start=True, stop=False))
        for kc in range(8):
            p.o("tensor", lambda e, tt=tt, kc=kc, bank=bank: e.matmul(
                pA[:, bank * 512: bank * 512 + 8],
                fm(xf32, kc, tt * 128, (tt + 1) * 128),
                rW[:, kc * 8:(kc + 1) * 8],
                start=False, stop=(kc == 7)),
                inc=("pe", 1) if kc == 7 else None)
        p.w("vector", "pe", p.cur("pe"))
        p.o("vector", lambda e, tt=tt, bank=bank: e.tensor_copy(
            lgt[:, tt * 8:(tt + 1) * 8], pA[:, bank * 512: bank * 512 + 8]),
            inc=("ve", 1))
        ev_rt[bank] = p.cur("ve")

    for tt in range(16):
        sl_ = slice(tt * 8, (tt + 1) * 8)
        p.w("vector", "ve", p.cur("ve"))
        p.o("vector", lambda e, sl_=sl_, tt=tt: e.reduce_max(
            gsc[:, tt:tt + 1], lgt[:, sl_], axis=AX), inc=("ve", 1))
        p.w("vector", "ve", p.cur("ve"))
        p.o("vector", lambda e, sl_=sl_, tt=tt: e.tensor_scalar(
            msk1[:, sl_], lgt[:, sl_], gsc[:, tt:tt + 1], None, OP.is_ge),
            inc=("ve", 1))
        p.w("vector", "ve", p.cur("ve"))
        p.o("vector", lambda e, sl_=sl_, tt=tt: e.scalar_tensor_tensor(
            msk2[:, sl_], msk1[:, sl_], -1e9, lgt[:, sl_], OP.mult, OP.add),
            inc=("ve", 1))
        p.w("vector", "ve", p.cur("ve"))
        p.o("vector", lambda e, sl_=sl_, tt=tt: e.reduce_max(
            gsc[:, 16 + tt:17 + tt], msk2[:, sl_], axis=AX), inc=("ve", 1))
    ve_m2 = p.cur("ve")
    p.w("vector", "ve", ve_m2)
    p.o("vector", lambda e: e.tensor_tensor(
        gsc[:, 32:48], gsc[:, 16:32], gsc[:, 0:16], OP.subtract), inc=("ve", 1))
    p.w("scalar", "ve", p.cur("ve"))
    p.o("scalar", lambda e: e.activation(gsc[:, 48:64], gsc[:, 32:48],
                                         AF.Sigmoid), inc=("se", 1))
    p.w("vector", "se", p.cur("se"))
    p.o("vector", lambda e: e.tensor_scalar(
        gsc[:, 32:48], gsc[:, 48:64], -1.0, 1.0, OP.mult, OP.add),
        inc=("ve", 1))
    for tt in range(16):
        sl_ = slice(tt * 8, (tt + 1) * 8)
        p.w("vector", "ve", p.cur("ve"))
        p.o("vector", lambda e, sl_=sl_, tt=tt: e.tensor_scalar(
            msk2[:, sl_], msk2[:, sl_], gsc[:, 16 + tt:17 + tt], None, OP.is_ge),
            inc=("ve", 1))
        p.w("vector", "ve", p.cur("ve"))
        p.o("vector", lambda e, sl_=sl_, tt=tt: e.tensor_scalar(
            gtok[:, sl_], msk1[:, sl_], gsc[:, 32 + tt:33 + tt], None, OP.mult),
            inc=("ve", 1))
        p.w("vector", "ve", p.cur("ve"))
        p.o("vector", lambda e, sl_=sl_, tt=tt: e.scalar_tensor_tensor(
            gtok[:, sl_], msk2[:, sl_], gsc[:, 48 + tt:49 + tt], gtok[:, sl_],
            OP.mult, OP.add), inc=("ve", 1))
    ve_gates = p.cur("ve")

    gtok16, iden16, pT1 = B["gtok16"], B["iden"], B["pT1"]
    p.w("vector", "ve", ve_gates)
    p.o("vector", lambda e: e.tensor_copy(gtok16[:, :], gtok[:, :]),
        inc=("ve", 1))
    p.w("tensor", "ve", p.cur("ve"))
    cp_done = 0
    for tt in range(16):
        p.w("tensor", "ve", cp_done)
        p.o("tensor", lambda e, tt=tt: e.transpose(
            pT1[0:8, 0:128], gtok16[:, tt * 8:(tt + 1) * 8], iden16[:, :]),
            inc=("pe", 1))
        p.w("vector", "pe", p.cur("pe"))
        p.o("vector", lambda e, tt=tt: e.tensor_copy(
            gT[:, tt * 128:(tt + 1) * 128], pT1[0:8, 0:128]), inc=("ve", 1))
        cp_done = p.cur("ve")

    ve_stats = emit_ln_stats(p, B)

    he_w1_pe = 0
    ev = {}
    dma_hist = []
    last_bcg_ve = 0
    for half in range(2):
        t0 = half * T
        g0 = half * 2
        oh = 1 - half

        def hv(kc, lo, hi, oh=oh):
            return xb16[:, kc * S + oh * 1024 + lo: kc * S + oh * 1024 + hi]

        p.w("vector", "pe", he_w1_pe)
        emit_ln_apply(p, B, ve_stats, mode="b16", g0=g0, g1=g0 + 2)
        ve_xhat = p.cur("ve")

        for c in range(NCH):
            for g in (g0, g0 + 1):
                bank = (c * 2 + g) % 4
                p.w("tensor", "ve", max(ev.get(("v", bank), 0), ve_xhat))
                p.w("tensor", "se", ev.get(("s", bank), 0))
                p.o("tensor", lambda e, c=c, g=g, bank=bank: e.matmul(
                    pA[:, bank * 512:(bank + 1) * 512],
                    b2[:, c * 128:(c + 1) * 128],
                    gT[:, g * 512:(g + 1) * 512], start=True, stop=True),
                    inc=("pe", 1))
                p.w("vector", "pe", p.cur("pe"))
                p.o("vector", lambda e, c=c, g=g, bank=bank:
                    e.scalar_tensor_tensor(
                        fm(xf32, c, g * 512, (g + 1) * 512),
                        fm(xf32, c, g * 512, (g + 1) * 512), 2.0,
                        pA[:, bank * 512:(bank + 1) * 512], OP.mult, OP.add),
                    inc=("ve", 1))
                ev[("v", bank)] = p.cur("ve")

        def emit_he(ex, t0=t0, oh=oh):
            p.w("vector", "pe", he_w1_pe)
            for c in range(NCH):
                p.o("vector", lambda e, c=c, ex=ex, oh=oh: e.tensor_scalar(
                    xb16[:, c * S + oh * 1024: c * S + oh * 1024 + 1024],
                    fm(xb16, c, t0, t0 + T),
                    lng[:, c * 8 + ex:c * 8 + ex + 1],
                    lnb[:, c * 8 + ex:c * 8 + ex + 1], OP.mult, OP.add),
                    inc=("ve", 1))
            return p.cur("ve")

        ve_he = emit_he(0)
        for ex in range(NE):
            p.w("tensor", "ve", p.cur("ve"))
            for gi in range(2):
                p.o("tensor", lambda e, ex=ex, t0=t0, gi=gi: e.matmul(
                    pB[:, gi * 512:(gi + 1) * 512],
                    sel[:, ex * 128:(ex + 1) * 128],
                    gT[:, t0 + gi * 512: t0 + (gi + 1) * 512],
                    start=True, stop=True),
                    inc=("pe", 1) if gi == 1 else None)
            p.w("vector", "pe", p.cur("pe"))
            for gi in range(2):
                p.o("vector", lambda e, gi=gi: e.tensor_copy(
                    bcG[:, gi * 512:(gi + 1) * 512],
                    pB[:, gi * 512:(gi + 1) * 512]), inc=("ve", 1))
            ve_bcg = p.cur("ve")
            last_bcg_ve = ve_bcg

            p.w("tensor", "ve", ve_he)
            for mb in range(8):
                if len(dma_hist) >= 2:
                    p.w("gpsimd", "pe", dma_hist[-2])
                wl = p.dma("gpsimd",
                           wbig[:, (mb % 2) * 4096:(mb % 2) * 4096 + 4096]
                           .rearrange("p (m k c) -> p m k c", m=4, k=8),
                           w1_hbm[ex, mb * 4:(mb + 1) * 4]
                           .rearrange("m k p c -> p m k c"), "wl")
                p.w("tensor", "wl", wl)
                for mi in range(4):
                    m = mb * 4 + mi
                    b0 = (m % 2) * 2
                    p.w("tensor", "se", max(ev.get(("s", b0), 0),
                                            ev.get(("s", b0 + 1), 0)))
                    p.w("tensor", "ve", max(ev.get(("v", b0), 0),
                                            ev.get(("v", b0 + 1), 0)))
                    for kc in range(8):
                        for gi in range(2):
                            p.o("tensor", lambda e, mb=mb, mi=mi, kc=kc, b0=b0,
                                gi=gi, hv=hv:
                                e.matmul(pA[:, (b0 + gi) * 512:(b0 + gi + 1) * 512],
                                         wbig[:, (mb % 2) * 4096 + mi * 1024 + kc * 128:
                                              (mb % 2) * 4096 + mi * 1024 + (kc + 1) * 128],
                                         hv(kc, gi * 512, (gi + 1) * 512),
                                         start=(kc == 0), stop=(kc == 7)),
                                inc=("pe", 1) if (kc == 7 and gi == 1) else None)
                    pe_mm = p.cur("pe")
                    p.w("scalar", "pe", pe_mm)
                    for gi in range(2):
                        p.o("scalar", lambda e, m=m, ex=ex, b0=b0, gi=gi:
                            e.activation(
                                ae[:, m * T + gi * 512: m * T + (gi + 1) * 512],
                                pA[:, (b0 + gi) * 512:(b0 + gi + 1) * 512],
                                AF.Gelu,
                                bias=b1t[:, ex * 32 + m:ex * 32 + m + 1]),
                            inc=("se", 1))
                    ev[("s", b0)] = ev[("s", b0 + 1)] = p.cur("se")
                dma_hist.append(p.cur("pe"))
            se_ae = p.cur("se")
            he_w1_pe = p.cur("pe")
            if ex + 1 < NE:
                ve_he = emit_he(ex + 1)

            p.w("tensor", "se", se_ae)
            p.w("tensor", "ve", ve_bcg)
            for m in range(NCH):
                if len(dma_hist) >= 2:
                    p.w("gpsimd", "pe", dma_hist[-2])
                wl = p.dma("gpsimd",
                           wbig[:, (m % 2) * 4096:(m % 2) * 4096 + 4096]
                           .rearrange("p (k c) -> p k c", k=32),
                           w2_hbm[ex, m].rearrange("k p c -> p k c"), "wl")
                p.w("tensor", "wl", wl)
                b0 = (m % 2) * 2
                p.w("tensor", "se", max(ev.get(("s", b0), 0),
                                        ev.get(("s", b0 + 1), 0)))
                p.w("tensor", "ve", max(ev.get(("v", b0), 0),
                                        ev.get(("v", b0 + 1), 0)))
                for kc in range(FCH):
                    for gi in range(2):
                        p.o("tensor", lambda e, m=m, kc=kc, b0=b0, gi=gi:
                            e.matmul(pA[:, (b0 + gi) * 512:(b0 + gi + 1) * 512],
                                     wbig[:, (m % 2) * 4096 + kc * 128:
                                          (m % 2) * 4096 + (kc + 1) * 128],
                                     ae[:, kc * T + gi * 512: kc * T + (gi + 1) * 512],
                                     start=(kc == 0), stop=(kc == FCH - 1)),
                            inc=("pe", 1) if (kc == FCH - 1 and gi == 1) else None)
                pe_mm = p.cur("pe")
                dma_hist.append(pe_mm)
                p.w("vector", "pe", pe_mm)
                for gi in range(2):
                    p.o("vector", lambda e, m=m, b0=b0, gi=gi: e.tensor_tensor(
                        pB[:, gi * 512:(gi + 1) * 512],
                        pA[:, (b0 + gi) * 512:(b0 + gi + 1) * 512],
                        bcG[:, gi * 512:(gi + 1) * 512], OP.mult), inc=("ve", 1))
                    p.w("vector", "ve", p.cur("ve"))
                    p.o("vector", lambda e, m=m, gi=gi, t0=t0: e.tensor_tensor(
                        fm(xf32, m, t0 + gi * 512, t0 + (gi + 1) * 512),
                        fm(xf32, m, t0 + gi * 512, t0 + (gi + 1) * 512),
                        pB[:, gi * 512:(gi + 1) * 512], OP.add), inc=("ve", 1))
                ev[("v", b0)] = ev[("v", b0 + 1)] = p.cur("ve")

    p.w("tensor", "ve", p.cur("ve"))
    emit_ln(p, B, "f32")


def alloc_common(nc, es, B):
    B["nc"] = nc
    B["xf32"] = es.enter_context(nc.sbuf_tensor("xf32", [128, NCH * S], F32))
    B["xb16"] = es.enter_context(nc.sbuf_tensor("xb16", [128, NCH * S], BF16))
    B["pA"] = es.enter_context(nc.psum_tensor("pA", [128, 2048], F32))
    B["pB"] = es.enter_context(nc.psum_tensor("pB", [128, 1024], F32))
    B["lnst"] = es.enter_context(nc.sbuf_tensor("lnst", [128, 2 * S], F32))
    B["onesk"] = es.enter_context(nc.sbuf_tensor("onesk", [128, 1], BF16))
    B["onesf"] = es.enter_context(nc.sbuf_tensor("onesf", [1, 128], F32))
    B["epsv"] = es.enter_context(nc.sbuf_tensor("epsv", [1, 1], F32))


def emit_prelude(p, B, xin_hbm):
    p.sem("sl", "wl", "pe", "ve", "se", "st")
    p.o("vector", lambda e: e.memset(B["onesk"][:, :], 1.0))
    p.o("vector", lambda e: e.memset(B["epsv"][:, :], EPS))
    p.o("vector", lambda e: e.memset(B["onesf"][:, :], 1.0), inc=("ve", 1))
    xf, xb = B["xf32"], B["xb16"]
    p.dma("sync", xf[:, :].rearrange("p (c t) -> p c t", c=NCH),
          xin_hbm.rearrange("(c p) t -> p c t", p=128), "sl")
    p.dma("gpsimd", xb[:, :].rearrange("p (c t) -> p c t", c=NCH),
          xin_hbm.rearrange("(c p) t -> p c t", p=128), "wl")
    sl_x = p.cur("sl")
    wl_x = p.cur("wl")
    p.w("tensor", "sl", sl_x)
    p.w("tensor", "wl", wl_x)
    p.w("tensor", "ve", 1)
    p.w("vector", "sl", sl_x)
    p.w("vector", "wl", wl_x)
    p.w("scalar", "sl", sl_x)
    p.w("scalar", "wl", wl_x)


def emit_store(p, B, xout_hbm, src=None):
    src = src if src is not None else B["xf32"]
    p.w("sync", "ve", p.cur("ve"))
    p.w("sync", "se", p.cur("se"))
    p.dma("sync", xout_hbm.rearrange("(c p) t -> p c t", p=128),
          src[:, :].rearrange("p (c t) -> p c t", c=NCH), "st")
    p.w("sync", "st", p.cur("st"))



def build_attn(layer):
    l1 = (layer == 1)
    nc = bass.Bass("TRN2", target_bir_lowering=False)
    xin = nc.dram_tensor("xin", [DM, S], F32, kind="ExternalInput").ap()
    wa = nc.dram_tensor("wa", [NPAIR, 3, 8, 128, 128], F32,
                        kind="ExternalInput").ap()
    wp = nc.dram_tensor("wp", [8, 8, 128, 128], F32, kind="ExternalInput").ap()
    bias_h = nc.dram_tensor("biast", [128, 32], F32, kind="ExternalInput").ap()
    iden_h = nc.dram_tensor("iden", [128, 128], BF16, kind="ExternalInput").ap()
    if l1:
        cscs_h = nc.dram_tensor("cscs", [128, S], BF16, kind="ExternalInput").ap()
        scsc_h = nc.dram_tensor("scsc", [128, S], BF16, kind="ExternalInput").ap()
        mneg_h = nc.dram_tensor("mneg", [128, 128], F32, kind="ExternalInput").ap()
        c1_h = nc.dram_tensor("c1m", [128, 128], BF16, kind="ExternalInput").ap()
        c2_h = nc.dram_tensor("c2m", [128, 128], BF16, kind="ExternalInput").ap()
    xout = nc.dram_tensor("xout", [DM, S], F32, kind="ExternalOutput").ap()

    es = ExitStack()
    B = {}
    alloc_common(nc, es, B)
    sb = nc.sbuf_tensor
    B["qkvp"] = es.enter_context(sb("qkvp", [128, 3 * S], BF16))
    B["vtok"] = es.enter_context(sb("vtok", [128, 2048], BF16))
    B["e16"] = es.enter_context(sb("e16", [128, 2 * S], BF16))
    B["attnT"] = es.enter_context(sb("attnT", [128, 2 * S], BF16))
    B["outT"] = es.enter_context(sb("outT", [128, NCH * S], BF16))
    B["sq16"] = B["outT"]
    B["tmp32"] = B["outT"][:, 0:1024].bitcast(F32)
    B["nmx"] = es.enter_context(sb("nmx", [128, 4], F32))
    B["sums"] = es.enter_context(sb("sums", [128, 8], F32))
    B["pmx"] = es.enter_context(sb("pmx", [128, 8], F32))
    B["stot"] = es.enter_context(sb("stot", [128, 2], F32))
    B["iden"] = es.enter_context(sb("idens", [128, 128], BF16))
    B["wbuf"] = es.enter_context(sb("wbuf", [128, 2 * 3072], BF16))
    B["wpbuf"] = es.enter_context(sb("wpbuf", [128, 2 * 1024], BF16))
    B["bias"] = es.enter_context(sb("biass", [128, 32], F32))
    B["pT1"] = es.enter_context(nc.psum_tensor("pT1", [128, 1024], BF16))
    B["pT2"] = es.enter_context(nc.psum_tensor("pT2", [128, 1024], BF16))
    if l1:
        B["cscs"] = es.enter_context(sb("cscss", [128, S], BF16))
        B["scsc"] = es.enter_context(sb("scscs", [128, S], BF16))
        B["maskneg"] = es.enter_context(sb("mnegs", [128, 128], F32))
        B["t1s"] = B["e16"][:, 0:S]
        B["t2s"] = B["e16"][:, S:2 * S]
        B["c1m"] = es.enter_context(sb("c1ms", [128, 128], BF16))
        B["c2m"] = es.enter_context(sb("c2ms", [128, 128], BF16))
    B["wa_hbm"] = wa
    B["wp_hbm"] = wp

    p = P()
    emit_prelude(p, B, xin)
    p.dma("sync", B["iden"][:, :], iden_h, "sl")
    p.dma("sync", B["bias"][:, :], bias_h, "sl")
    if l1:
        p.dma("sync", B["cscs"][:, :], cscs_h, "sl")
        p.dma("sync", B["scsc"][:, :], scsc_h, "sl")
        p.dma("sync", B["maskneg"][:, :], mneg_h, "sl")
        p.dma("sync", B["c1m"][:, :], c1_h, "sl")
        p.dma("sync", B["c2m"][:, :], c2_h, "sl")
    B["sl_done"] = p.cur("sl")
    p.w("tensor", "sl", B["sl_done"])
    emit_attention(p, B, layer)
    emit_store(p, B, xout)
    p.emit(nc, es)
    es.close()
    return nc


def build_l0_ffn():
    nc = bass.Bass("TRN2", target_bir_lowering=False)
    xin = nc.dram_tensor("xin", [DM, S], F32, kind="ExternalInput").ap()
    w12 = nc.dram_tensor("w12", [FCH, 2, 8, 128, 128], F32,
                         kind="ExternalInput").ap()
    h1w = nc.dram_tensor("h1w", [FCH, FCH, 128, 128], F32,
                         kind="ExternalInput").ap()
    h2w = nc.dram_tensor("h2w", [FCH, FCH, 128, 128], F32,
                         kind="ExternalInput").ap()
    w3w = nc.dram_tensor("w3w", [8, FCH, 128, 128], F32,
                         kind="ExternalInput").ap()
    scl_h = nc.dram_tensor("scl", [128, 144], F32, kind="ExternalInput").ap()
    xout = nc.dram_tensor("xout", [DM, S], F32, kind="ExternalOutput").ap()

    es = ExitStack()
    B = {}
    alloc_common(nc, es, B)
    sb = nc.sbuf_tensor
    h0h = es.enter_context(sb("h0", [128, FCH * 1024], BF16))
    B["h0"] = h0h[:, :]
    B["h1bc"] = B["xf32"][:, :].bitcast(BF16)
    B["sq16"] = h0h[:, 0:NCH * S]
    B["tmp32"] = h0h[:, 0:1024].bitcast(F32)
    B["a_scr"] = B["lnst"][:, :].bitcast(BF16)[:, 0:2048]
    B["wbig"] = es.enter_context(sb("wbig", [128, 2 * 4096], BF16))
    B["scl"] = es.enter_context(sb("scls", [128, 144], F32))
    B["xsp_hbm"] = nc.dram_tensor("xsp", [128, NCH * S], F32,
                                  kind="Internal").ap()
    B["w12_hbm"], B["h1_hbm"], B["h2_hbm"], B["w3_hbm"] = w12, h1w, h2w, w3w

    p = P()
    emit_prelude(p, B, xin)
    p.dma("sync", B["scl"][:, :], scl_h, "sl")
    B["sl_done"] = p.cur("sl")
    p.w("tensor", "sl", B["sl_done"])
    p.w("scalar", "sl", B["sl_done"])
    p.w("vector", "sl", B["sl_done"])
    emit_l0_ffn(p, B)
    emit_store(p, B, xout)
    p.emit(nc, es)
    es.close()
    return nc


def build_l1_ffn():
    nc = bass.Bass("TRN2", target_bir_lowering=False)
    xin = nc.dram_tensor("xin", [DM, S], F32, kind="ExternalInput").ap()
    w1 = nc.dram_tensor("w1", [NE, FCH, 8, 128, 128], F32,
                        kind="ExternalInput").ap()
    w2 = nc.dram_tensor("w2", [NE, NCH, FCH, 128, 128], F32,
                        kind="ExternalInput").ap()
    rw_h = nc.dram_tensor("rw", [128, 64], F32, kind="ExternalInput").ap()
    rb_h = nc.dram_tensor("rb", [1, 8], F32, kind="ExternalInput").ap()
    lng_h = nc.dram_tensor("lng", [128, 64], F32, kind="ExternalInput").ap()
    lnb_h = nc.dram_tensor("lnb", [128, 64], F32, kind="ExternalInput").ap()
    b1_h = nc.dram_tensor("b1", [128, 256], F32, kind="ExternalInput").ap()
    b2_h = nc.dram_tensor("b2", [8, 1024], F32, kind="ExternalInput").ap()
    sel_h = nc.dram_tensor("sel", [8, 1024], F32, kind="ExternalInput").ap()
    idf_h = nc.dram_tensor("idf", [128, 128], F32, kind="ExternalInput").ap()
    xout = nc.dram_tensor("xout", [DM, S], F32, kind="ExternalOutput").ap()

    es = ExitStack()
    B = {}
    alloc_common(nc, es, B)
    sb = nc.sbuf_tensor
    B["rw_sb"] = es.enter_context(sb("rws", [128, 64], F32))
    B["rb_sb"] = es.enter_context(sb("rbs", [1, 8], F32))
    B["lgt"] = es.enter_context(sb("lgt", [128, 128], F32))
    B["gtok"] = B["lgt"][:, :]
    B["gsc"] = es.enter_context(sb("gsc", [128, 64], F32))
    B["msk1"] = es.enter_context(sb("msk1", [128, 128], BF16))
    B["msk2"] = es.enter_context(sb("msk2", [128, 128], BF16))
    B["gtok16"] = B["msk1"][:, :]
    B["iden"] = es.enter_context(sb("iden16", [128, 128], BF16))
    B["lng_sb"] = es.enter_context(sb("lngs", [128, 64], F32))
    B["lnb_sb"] = es.enter_context(sb("lnbs", [128, 64], F32))
    B["b1_sb"] = es.enter_context(sb("b1s", [128, 256], F32))
    aeh = es.enter_context(sb("ae", [128, FCH * 1024], BF16))
    B["ae"] = aeh[:, :]
    B["sq16"] = aeh[:, 0:NCH * S]
    B["tmp32"] = aeh[:, 0:1024].bitcast(F32)
    B["bcG"] = es.enter_context(sb("bcG", [128, 512], F32))[:, :].bitcast(BF16)
    B["wbig"] = es.enter_context(sb("wbig", [128, 2 * 4096], BF16))
    B["pT1"] = es.enter_context(nc.psum_tensor("pT1", [128, 1024], BF16))
    GTSl = es.enter_context(sb("GTS", [128, 4096], BF16))
    B["gT"] = GTSl[0:8, 0:2048]
    B["sel"] = GTSl[0:8, 2048:3072]
    B["b2_sb"] = GTSl[0:8, 3072:4096]
    B["w1_hbm"], B["w2_hbm"] = w1, w2

    p = P()
    emit_prelude(p, B, xin)
    for nm, hb in (("rw_sb", rw_h), ("rb_sb", rb_h), ("lng_sb", lng_h),
                   ("lnb_sb", lnb_h), ("b1_sb", b1_h)):
        p.dma("sync", B[nm][:, :], hb, "sl")
    p.dma("gpsimd", B["iden"][:, :], idf_h, "wl")
    p.dma("gpsimd", B["b2_sb"], b2_h, "wl")
    p.dma("gpsimd", B["sel"], sel_h, "wl")
    B["sl_done"] = p.cur("sl")
    wl_c = p.cur("wl")
    p.w("tensor", "sl", B["sl_done"])
    p.w("tensor", "wl", wl_c)
    p.w("scalar", "sl", B["sl_done"])
    p.w("vector", "sl", B["sl_done"])
    emit_l1_ffn(p, B)
    emit_store(p, B, xout)
    p.emit(nc, es)
    es.close()
    return nc



def _tile_kxm(w):
    K, M = w.shape
    return np.ascontiguousarray(
        w.reshape(K // 128, 128, M // 128, 128).transpose(2, 0, 1, 3))


def prep_l0_attn(I):
    w = np.asarray(I["L0_attn_w_qkv"])
    pairs = []
    for pp in range(NPAIR):
        cols = []
        for t3 in range(3):
            blk = w[:, t3 * 1024 + pp * 128: t3 * 1024 + (pp + 1) * 128]
            cols.append(blk.reshape(8, 128, 128))
        pairs.append(np.stack(cols))
    wa = np.ascontiguousarray(np.stack(pairs))
    wp = _tile_kxm(np.asarray(I["L0_attn_w_proj"]))
    bias = np.zeros((128, 32), np.float32)
    bq = np.asarray(I["L0_attn_b_qkv"]).reshape(24, 128).T
    bias[:, 0:24] = bq
    bias[:, 24:32] = np.asarray(I["L0_attn_b_proj"]).reshape(8, 128).T
    return {"wa": wa, "wp": wp, "biast": np.ascontiguousarray(bias),
            "iden": np.eye(128, dtype=BF)}


def _rope_tables():
    inv = 1.0 / 10000.0 ** (np.arange(0, DK, 2, dtype=np.float64) / DK)
    pos = np.arange(S, dtype=np.float64)
    fr = np.outer(pos, inv)
    c = np.cos(fr).T.astype(np.float32)
    s = np.sin(fr).T.astype(np.float32)
    cscs = np.concatenate([c, s, c, s]).astype(BF)
    scsc = np.concatenate([s, c, s, c]).astype(BF)
    return np.ascontiguousarray(cscs), np.ascontiguousarray(scsc)


def prep_l1_attn(I):
    perm = np.concatenate([h * 64 + np.concatenate(
        [np.arange(0, 64, 2), np.arange(1, 64, 2)]) for h in range(H)])
    Wq = np.asarray(I["L1_attn_Wq"])[:, perm]
    Wk = np.asarray(I["L1_attn_Wk"])[:, perm]
    Wv = np.asarray(I["L1_attn_Wv"])
    pairs = []
    for pp in range(NPAIR):
        cols = [W[:, pp * 128:(pp + 1) * 128].reshape(8, 128, 128)
                for W in (Wq, Wk, Wv)]
        pairs.append(np.stack(cols))
    wa = np.ascontiguousarray(np.stack(pairs))
    wp = _tile_kxm(np.asarray(I["L1_attn_Wo"]))
    bias = np.zeros((128, 32), np.float32)
    bias[:, 0:8] = np.asarray(I["L1_attn_bq"])[perm].reshape(8, 128).T
    bias[:, 8:16] = np.asarray(I["L1_attn_bk"])[perm].reshape(8, 128).T
    bias[:, 16:24] = np.asarray(I["L1_attn_bv"]).reshape(8, 128).T
    bias[:, 24:32] = np.asarray(I["L1_attn_bo"]).reshape(8, 128).T
    cscs, scsc = _rope_tables()
    mneg = np.triu(np.full((128, 128), -1e9, np.float32), 1).astype(BF)
    c1 = np.zeros((128, 128), np.float32)
    c2 = np.zeros((128, 128), np.float32)
    for b0 in (0, 64):
        for m in range(32):
            c1[b0 + m, b0 + m] = 1.0
            c1[b0 + 32 + m, b0 + m] = -1.0
            c2[b0 + m, b0 + 32 + m] = 1.0
            c2[b0 + 32 + m, b0 + 32 + m] = 1.0
    return {"wa": wa, "wp": wp, "biast": np.ascontiguousarray(bias),
            "iden": np.eye(128, dtype=BF), "cscs": cscs, "scsc": scsc,
            "mneg": np.ascontiguousarray(mneg),
            "c1m": np.ascontiguousarray(c1.astype(BF)),
            "c2m": np.ascontiguousarray(c2.astype(BF))}


def prep_l0_ffn(I):
    w1T = np.asarray(I["L0_ffn_w1_w"]).T
    w2T = np.asarray(I["L0_ffn_w2_w"]).T
    w12 = np.stack([_tile_kxm(w1T), _tile_kxm(w2T)], axis=1)
    h1w = _tile_kxm(np.asarray(I["L0_ffn_h1_w"]).T)
    h2w = _tile_kxm(np.asarray(I["L0_ffn_h2_w"]).T)
    w3w = _tile_kxm(np.asarray(I["L0_ffn_w3_w"]).T)
    scl = np.zeros((128, 144), np.float32)
    scl[:, 0:32] = np.asarray(I["L0_ffn_w1_s"]).reshape(32, 128).T
    scl[:, 32:64] = np.asarray(I["L0_ffn_w2_s"]).reshape(32, 128).T
    scl[:, 64:96] = np.asarray(I["L0_ffn_h1_s"]).reshape(32, 128).T
    scl[:, 96:128] = np.asarray(I["L0_ffn_h2_s"]).reshape(32, 128).T
    scl[:, 128:136] = np.asarray(I["L0_ffn_w3_s"]).reshape(8, 128).T
    return {"w12": np.ascontiguousarray(w12), "h1w": h1w, "h2w": h2w,
            "w3w": w3w, "scl": np.ascontiguousarray(scl)}


def prep_l1_ffn(I):
    W1 = np.asarray(I["L1_ffn_W1"])
    W2 = np.asarray(I["L1_ffn_W2"])
    w1 = np.stack([_tile_kxm(W1[e]) for e in range(NE)])
    w2 = np.stack([_tile_kxm(W2[e]) for e in range(NE)])
    rw = np.asarray(I["L1_ffn_router_W"]).reshape(8, 128, 8)
    rw = np.ascontiguousarray(rw.transpose(1, 0, 2).reshape(128, 64))
    rb = np.asarray(I["L1_ffn_router_b"]).reshape(1, 8)
    lng = np.asarray(I["L1_ffn_ln_g"]).T.reshape(8, 128, 8)
    lng = np.ascontiguousarray(lng.transpose(1, 0, 2).reshape(128, 64))
    lnb = np.asarray(I["L1_ffn_ln_b"]).T.reshape(8, 128, 8)
    lnb = np.ascontiguousarray(lnb.transpose(1, 0, 2).reshape(128, 64))
    b1 = np.asarray(I["L1_ffn_b1"]).reshape(8, 32, 128)
    b1 = np.ascontiguousarray(b1.transpose(2, 0, 1).reshape(128, 256))
    b2 = np.ascontiguousarray(np.asarray(I["L1_ffn_b2"]).astype(np.float32))
    sel = np.zeros((8, 1024), np.float32)
    for e in range(NE):
        sel[e, e * 128:(e + 1) * 128] = 1.0
    return {"w1": np.ascontiguousarray(w1), "w2": np.ascontiguousarray(w2),
            "rw": rw, "rb": np.ascontiguousarray(rb.astype(np.float32)),
            "lng": lng, "lnb": lnb, "b1": b1, "b2": b2, "sel": sel,
            "idf": np.eye(128, dtype=np.float32)}



_NC_CACHE = {}


def _get_nc(key):
    if key not in _NC_CACHE:
        if key == "l0a":
            _NC_CACHE[key] = build_attn(0)
        elif key == "l1a":
            _NC_CACHE[key] = build_attn(1)
        elif key == "l0f":
            _NC_CACHE[key] = build_l0_ffn()
        elif key == "l1f":
            _NC_CACHE[key] = build_l1_ffn()
    return _NC_CACHE[key]


def _run(key, per_core_const, xts):
    nc = _get_nc(key)
    in_maps = []
    for b in range(8):
        m = dict(per_core_const)
        m["xin"] = xts[b]
        in_maps.append(m)
    res = run_bass_kernel_spmd(nc, in_maps, core_ids=list(range(8)))
    return [np.asarray(res.results[b]["xout"]) for b in range(8)]


def _fence(p):
    for nm in ("pe", "ve", "se", "wl", "sl"):
        v = p.cur(nm)
        for eng in ENGS:
            p.w(eng, nm, v)


def build_fused():
    nc = bass.Bass("TRN2", target_bir_lowering=False)
    d = nc.dram_tensor
    xin = d("xin", [DM, S], F32, kind="ExternalInput").ap()
    xout = d("xout", [DM, S], F32, kind="ExternalOutput").ap()
    hb = {
        "wa0": d("wa0", [NPAIR, 3, 8, 128, 128], F32, kind="ExternalInput").ap(),
        "wp0": d("wp0", [8, 8, 128, 128], F32, kind="ExternalInput").ap(),
        "biast0": d("biast0", [128, 32], F32, kind="ExternalInput").ap(),
        "iden": d("iden", [128, 128], BF16, kind="ExternalInput").ap(),
        "w12": d("w12", [FCH, 2, 8, 128, 128], F32, kind="ExternalInput").ap(),
        "h1w": d("h1w", [FCH, FCH, 128, 128], F32, kind="ExternalInput").ap(),
        "h2w": d("h2w", [FCH, FCH, 128, 128], F32, kind="ExternalInput").ap(),
        "w3w": d("w3w", [8, FCH, 128, 128], F32, kind="ExternalInput").ap(),
        "scl": d("scl", [128, 144], F32, kind="ExternalInput").ap(),
        "wa1": d("wa1", [NPAIR, 3, 8, 128, 128], F32, kind="ExternalInput").ap(),
        "wp1": d("wp1", [8, 8, 128, 128], F32, kind="ExternalInput").ap(),
        "biast1": d("biast1", [128, 32], F32, kind="ExternalInput").ap(),
        "cscs": d("cscs", [128, S], BF16, kind="ExternalInput").ap(),
        "scsc": d("scsc", [128, S], BF16, kind="ExternalInput").ap(),
        "mneg": d("mneg", [128, 128], BF16, kind="ExternalInput").ap(),
        "c1m": d("c1m", [128, 128], BF16, kind="ExternalInput").ap(),
        "c2m": d("c2m", [128, 128], BF16, kind="ExternalInput").ap(),
        "w1e": d("w1e", [NE, FCH, 8, 128, 128], F32, kind="ExternalInput").ap(),
        "w2e": d("w2e", [NE, NCH, FCH, 128, 128], F32, kind="ExternalInput").ap(),
        "rw": d("rw", [128, 64], F32, kind="ExternalInput").ap(),
        "rb": d("rb", [1, 8], F32, kind="ExternalInput").ap(),
        "lng": d("lng", [128, 64], F32, kind="ExternalInput").ap(),
        "lnb": d("lnb", [128, 64], F32, kind="ExternalInput").ap(),
        "b1h": d("b1h", [128, 256], F32, kind="ExternalInput").ap(),
        "b2h": d("b2h", [8, 1024], F32, kind="ExternalInput").ap(),
        "selh": d("selh", [8, 1024], BF16, kind="ExternalInput").ap(),
        "idf": d("idf", [128, 128], F32, kind="ExternalInput").ap(),
    }

    es = ExitStack()
    sb = nc.sbuf_tensor
    C = {"nc": nc}
    C["xf32"] = es.enter_context(sb("xf32", [128, NCH * S], F32))
    C["xb16"] = es.enter_context(sb("xb16", [128, NCH * S], BF16))
    C["pA"] = es.enter_context(nc.psum_tensor("pA", [128, 2048], F32))
    C["pB"] = es.enter_context(nc.psum_tensor("pB", [128, 1024], F32))
    C["pT1"] = es.enter_context(nc.psum_tensor("pT1", [128, 1024], BF16))
    C["pT2"] = es.enter_context(nc.psum_tensor("pT2", [128, 1024], BF16))
    C["lnst"] = es.enter_context(sb("lnst", [128, 2 * S], F32))
    C["onesk"] = es.enter_context(sb("onesk", [128, 1], BF16))
    C["onesf"] = es.enter_context(sb("onesf", [1, 128], F32))
    C["epsv"] = es.enter_context(sb("epsv", [1, 1], F32))
    C["iden"] = es.enter_context(sb("idens", [128, 128], BF16))
    C["nmx"] = es.enter_context(sb("nmx", [128, 4], F32))
    C["sums"] = es.enter_context(sb("sums", [128, 8], F32))
    C["pmx"] = es.enter_context(sb("pmx", [128, 8], F32))
    C["stot"] = es.enter_context(sb("stot", [128, 2], F32))
    A16 = es.enter_context(sb("A16", [128, 41216], BF16))
    A32 = es.enter_context(sb("A32", [128, 1368], F32))
    GTS = es.enter_context(sb("GTS", [128, 4096], BF16))
    C["mneg16"] = es.enter_context(sb("mneg16", [128, 128], BF16))
    C["msk1b"] = es.enter_context(sb("msk1b", [128, 128], BF16))
    C["msk2b"] = C["mneg16"]
    C["gtok16"] = C["msk1b"]

    def v16(c0, n):
        return A16[:, c0:c0 + n]

    def v32(c0, n):
        return A32[:, c0:c0 + n]

    LGT = 0
    GSC, BIAS0, BIAS1 = 128, 192, 224
    SCL, B1T, LNG, LNB, RW, RB, BCG = 256, 400, 656, 720, 784, 848, 856

    p = P()
    emit_prelude(p, C, xin)
    for ap_sb, ap_hb in ((C["iden"][:, :], hb["iden"]), (v32(BIAS0, 32), hb["biast0"]),
                         (v32(BIAS1, 32), hb["biast1"]),
                         (C["mneg16"][:, :], hb["mneg"]),
                         (v32(SCL, 144), hb["scl"]), (v32(B1T, 256), hb["b1h"]),
                         (v32(LNG, 64), hb["lng"]), (v32(LNB, 64), hb["lnb"]),
                         (v32(RW, 64), hb["rw"])):
        p.dma("sync", ap_sb, ap_hb, "sl")
    p.dma("sync", A32[0:1, RB:RB + 8], hb["rb"], "sl")
    sl0 = p.cur("sl")

    Ba = dict(C)
    Ba.update(qkvp=v16(0, 3 * S), vtok=v16(6144, 2048), e16=v16(8192, 2 * S),
              attnT=v16(12288, 2 * S), wbuf=v16(16384, 3072),
              wpbuf=v16(19456, 1024), outT=v16(20480, 8 * S),
              sq16=v16(20480, 8 * S), bias=v32(BIAS0, 32),
              tmp32=v16(20480, 1024).bitcast(F32),
              wa_hbm=hb["wa0"], wp_hbm=hb["wp0"], sl_done=sl0, wslots=1)
    p.w("tensor", "sl", sl0)
    emit_attention(p, Ba, 0)
    _fence(p)

    Bf = dict(C)
    Bf.update(h0=v16(0, 32768), h1bc=C["xf32"][:, :].bitcast(BF16),
              sq16=v16(0, NCH * S), tmp32=v16(0, 1024).bitcast(F32),
              a_scr=C["lnst"][:, :].bitcast(BF16)[:, 0:2048],
              wbig=v16(32768, 8192), scl=v32(SCL, 144),
              xsp_hbm=nc.dram_tensor("xsp", [128, NCH * S], F32,
                                     kind="Internal").ap(),
              w12_hbm=hb["w12"], h1_hbm=hb["h1w"], h2_hbm=hb["h2w"],
              w3_hbm=hb["w3w"], sl_done=sl0)
    emit_l0_ffn(p, Bf)
    _fence(p)

    p.dma("sync", v16(36864, 2048), hb["cscs"], "sl")
    p.dma("sync", v16(38912, 2048), hb["scsc"], "sl")
    p.dma("sync", v16(40960, 128), hb["c1m"], "sl")
    p.dma("sync", v16(41088, 128), hb["c2m"], "sl")
    sl1 = p.cur("sl")
    Bb = dict(C)
    Bb.update(qkvp=v16(0, 3 * S), vtok=v16(6144, 2048), e16=v16(8192, 2 * S),
              attnT=v16(12288, 2 * S), wbuf=v16(16384, 3072),
              wpbuf=v16(19456, 1024), outT=v16(20480, 8 * S),
              sq16=v16(20480, 8 * S), bias=v32(BIAS1, 32),
              tmp32=v16(20480, 1024).bitcast(F32),
              cscs=v16(36864, 2048), scsc=v16(38912, 2048),
              c1m=v16(40960, 128), c2m=v16(41088, 128),
              t1s=v16(8192, 2048), t2s=v16(10240, 2048),
              maskneg=C["mneg16"][:, :],
              wa_hbm=hb["wa1"], wp_hbm=hb["wp1"], sl_done=sl1, wslots=1)
    p.w("tensor", "sl", sl1)
    p.w("vector", "sl", sl1)
    p.w("scalar", "sl", sl1)
    emit_attention(p, Bb, 1)
    _fence(p)

    Bm = dict(C)
    Bm.update(ae=v16(0, 32768), sq16=v16(0, 16384),
              tmp32=v16(0, 1024).bitcast(F32),
              wbig=v16(32768, 8192), gT=GTS[0:8, 0:2048],
              sel=GTS[0:8, 2048:3072], b2_sb=GTS[0:8, 3072:4096],
              rw_sb=v32(RW, 64), rb_sb=A32[0:1, RB:RB + 8],
              gtok=v32(LGT, 128), lgt=v32(LGT, 128),
              gsc=v32(GSC, 64), msk1=C["msk1b"][:, :], msk2=C["msk2b"][:, :],
              lng_sb=v32(LNG, 64), lnb_sb=v32(LNB, 64), b1_sb=v32(B1T, 256),
              bcG=v32(BCG, 512).bitcast(BF16), w1_hbm=hb["w1e"],
              w2_hbm=hb["w2e"], sl_done=sl0)
    p.dma("sync", Bm["sel"], hb["selh"], "sl")
    p.dma("gpsimd", Bm["b2_sb"], hb["b2h"], "wl")
    Bm["sl_done"] = p.cur("sl")
    wl_l1f = p.cur("wl")
    p.w("vector", "sl", Bm["sl_done"])
    p.w("scalar", "sl", Bm["sl_done"])
    p.w("tensor", "wl", wl_l1f)
    emit_l1_ffn(p, Bm)
    emit_store(p, C, xout)
    p.emit(nc, es)
    es.close()
    return nc


def prep_fused(I):
    a0 = prep_l0_attn(I)
    f0 = prep_l0_ffn(I)
    a1 = prep_l1_attn(I)
    f1 = prep_l1_ffn(I)
    sel16 = np.zeros((8, 1024), BF)
    for e in range(NE):
        sel16[e, e * 128:(e + 1) * 128] = 1.0
    return {
        "wa0": a0["wa"], "wp0": a0["wp"], "biast0": a0["biast"],
        "iden": a0["iden"],
        "w12": f0["w12"], "h1w": f0["h1w"], "h2w": f0["h2w"],
        "w3w": f0["w3w"], "scl": f0["scl"],
        "wa1": a1["wa"], "wp1": a1["wp"], "biast1": a1["biast"],
        "cscs": a1["cscs"], "scsc": a1["scsc"], "mneg": a1["mneg"],
        "c1m": a1["c1m"], "c2m": a1["c2m"],
        "w1e": f1["w1"], "w2e": f1["w2"], "rw": f1["rw"], "rb": f1["rb"],
        "lng": f1["lng"], "lnb": f1["lnb"], "b1h": f1["b1"], "b2h": f1["b2"],
        "selh": sel16, "idf": f1["idf"],
    }


def kernel(**I):
    x = np.asarray(I["L0_attn_x"])
    xts = [np.ascontiguousarray(x[b].T) for b in range(8)]
    if "fused" not in _NC_CACHE:
        _NC_CACHE["fused"] = build_fused()
    nc = _NC_CACHE["fused"]
    const = prep_fused(I)
    in_maps = []
    for b in range(8):
        m = dict(const)
        m["xin"] = xts[b]
        in_maps.append(m)
    res = run_bass_kernel_spmd(nc, in_maps, core_ids=list(range(8)))
    out = np.stack([np.asarray(res.results[b]["xout"]).T for b in range(8)])
    return np.ascontiguousarray(out.astype(np.float32))


def kernel_unfused(**I):
    x = np.asarray(I["L0_attn_x"])
    xts = [np.ascontiguousarray(x[b].T) for b in range(8)]

    c0a = prep_l0_attn(I)
    xts = _run("l0a", c0a, xts)
    c0f = prep_l0_ffn(I)
    xts = _run("l0f", c0f, xts)
    c1a = prep_l1_attn(I)
    xts = _run("l1a", c1a, xts)
    c1f = prep_l1_ffn(I)
    xts = _run("l1f", c1f, xts)

    out = np.stack([xts[b].T for b in range(8)])
    return np.ascontiguousarray(out.astype(np.float32))
